# version 02 (final)
# baseline (speedup 1.0000x reference)
import numpy as np
from contextlib import ExitStack
import concourse.bass as bass
import concourse.mybir as mybir
from concourse.alu_op_type import AluOpType
from concourse.bass_utils import run_bass_kernel_spmd

F32 = mybir.dt.float32
BF16 = mybir.dt.bfloat16
I32 = mybir.dt.int32
U8 = mybir.dt.uint8
AF = mybir.ActivationFunctionType
AX = mybir.AxisListType
OP = AluOpType

D = 2048
NCH = D // 128
T_OWN = 1024
NCORES = 8

ENGINES = ("sync", "vector", "scalar", "gpsimd", "tensor")


class Sem:
    def __init__(self, h):
        self.h = h
        self.c = 0

    def bump(self, n):
        self.c += n
        return self.c


class Prog:
    CHAINED = ("vector", "scalar", "gpsimd")

    def __init__(self):
        self.nc = bass.Bass("TRN2", target_bir_lowering=False)
        self.stack = ExitStack()
        self.q = {e: [] for e in ENGINES}
        self._names = set()
        self.sems = {e: self.sem(f"bar_{e}") for e in ENGINES}
        self.chain = {e: self.sem(f"ch_{e}") for e in self.CHAINED}

    def _nm(self, name):
        base, i = name, 0
        while name in self._names:
            i += 1
            name = f"{base}_{i}"
        self._names.add(name)
        return name

    def sem(self, name):
        return Sem(self.stack.enter_context(self.nc.semaphore(self._nm(name))))

    def sb(self, name, shape, dt):
        return self.stack.enter_context(self.nc.sbuf_tensor(self._nm(name), shape, dt))

    def ps(self, name, shape, dt=F32):
        return self.stack.enter_context(self.nc.psum_tensor(self._nm(name), shape, dt))

    def dram(self, name, shape, dt, kind="Internal"):
        return self.nc.dram_tensor(name, shape, dt, kind=kind)

    def on(self, eng, fn, wait=(), inc=None):
        for s, v in wait:
            self.q[eng].append(lambda e, s=s, v=v: e.wait_ge(s.h, v))
        if eng in self.CHAINED:
            ch = self.chain[eng]
            self.q[eng].append(lambda e, s=ch, v=ch.c: e.wait_ge(s.h, v))
            if inc is None:
                inc = (ch, 1)
        if inc is not None:
            s, n = inc
            s.bump(n)
            self.q[eng].append(lambda e, fn=fn, s=s, n=n: fn(e).then_inc(s.h, n))
        else:
            self.q[eng].append(fn)

    def wait(self, eng, sem, v=None):
        v = sem.c if v is None else v
        self.q[eng].append(lambda e, s=sem, v=v: e.wait_ge(s.h, v))

    def barrier(self):
        for e in ENGINES:
            self.on(e, lambda en: en.nop(), inc=(self.sems[e], 1))
        targets = {e: self.sems[e].c for e in ENGINES}
        for e in ENGINES:
            for oe in ENGINES:
                if oe != e:
                    self.wait(e, self.sems[oe], targets[oe])

    def build(self):
        with self.nc.Block() as block:
            for e in ENGINES:
                def body(eng, q=self.q[e]):
                    for fn in q:
                        fn(eng)
                getattr(block, e)(body)
        self.stack.close()
        return self.nc



class Common:

    def __init__(self, p, T, rows_n=6):
        self.p = p
        self.T = T
        self.rows_n = rows_n
        self.ones_col = p.sb("ones_col", [128, 1], F32)
        self.ones_row = p.sb("ones_row", [1, 128], F32)
        self.ident = p.sb("ident", [128, 128], F32)
        self.identb = p.sb("identb", [128, 128], BF16)
        self.rows = (p.sb("rows", [1, rows_n * T], F32) if rows_n else None)
        self._rows_used = 0
        self.v = p.chain["vector"]
        self.s = p.chain["scalar"]
        self.t = p.sem("t")
        self.g = p.chain["gpsimd"]
        self.dl = p.sem("dl")
        self.dw = p.sem("dw")
        self.db = p.sem("db")
        self.do = p.sem("do")
        self.bounce = p.dram("bounce", [4096], F32)

    def row(self, n=1):
        r = self.rows[0:1, self._rows_used * self.T:(self._rows_used + n) * self.T]
        self._rows_used += n
        assert self._rows_used <= self.rows_n
        return r

    def row_at(self, slot):
        return self.rows[0:1, slot * self.T:(slot + 1) * self.T]

    def init_consts(self, identity_input_ap=None):
        p = self.p
        p.on("vector", lambda e: e.memset(self.ones_col[:, :], 1.0))
        p.on("vector", lambda e: e.memset(self.ones_row[:, :], 1.0), inc=(self.v, 1))
        if identity_input_ap is not None:
            di = p.sem("di")
            p.on("sync", lambda e: e.dma_start(out=self.ident[:, :],
                                               in_=identity_input_ap[:, :]), inc=(di, 16))
            p.wait("vector", di)
            p.on("vector", lambda e: e.tensor_copy(out=self.identb[:, :],
                                                   in_=self.ident[:, :]), inc=(self.v, 1))
            self.di = di


def fm_load(p, cm, tiles, dram_ap, T, sem=None, qeng="sync"):
    sem = sem or cm.dl
    for ci in range(len(tiles)):
        p.on(qeng, lambda e, ci=ci: e.dma_start(
            out=tiles[ci][:, 0:T], in_=dram_ap[ci * 128:(ci + 1) * 128, 0:T]),
            inc=(sem, 16))
    return sem.c


def fm_store(p, cm, dram_ap, tiles, T, qeng="sync"):
    for ci in range(len(tiles)):
        p.on(qeng, lambda e, ci=ci: e.dma_start(
            out=dram_ap[ci * 128:(ci + 1) * 128, 0:T], in_=tiles[ci][:, 0:T]),
            inc=(cm.do, 16))
    return cm.do.c


def stats_sums(p, cm, x_tiles, T, ps_list, sq_tile, want_sq=True, wait_first=()):
    nr = (T + 511) // 512
    nch = len(x_tiles)
    for r in range(nr):
        n = min(512, T - r * 512)
        for ci in range(nch):
            p.on("tensor", lambda e, r=r, n=n, ci=ci: e.matmul(
                ps_list[r][0:1, 0:n], cm.ones_col[:, :],
                x_tiles[ci][:, r * 512:r * 512 + n],
                start=(ci == 0), stop=(ci == nch - 1)),
                wait=(wait_first if (ci == 0 and r == 0) else ()),
                inc=((cm.t, 1) if ci == nch - 1 else None))
    if want_sq:
        for r in range(nr):
            n = min(512, T - r * 512)
            for ci in range(nch):
                buf = sq_tile[ci % 2]
                p.on("scalar", lambda e, buf=buf, r=r, n=n, ci=ci: e.activation(
                    out=buf[:, 0:n], in_=x_tiles[ci][:, r * 512:r * 512 + n],
                    func=AF.Square),
                    wait=([(cm.t, cm.t.c)] if ci >= 2 else ()),
                    inc=(cm.s, 1))
                p.wait("tensor", cm.s)
                p.on("tensor", lambda e, buf=buf, r=r, n=n, ci=ci: e.matmul(
                    ps_list[nr + r][0:1, 0:n], cm.ones_col[:, :], buf[:, 0:n],
                    start=(ci == 0), stop=(ci == nch - 1)),
                    inc=(cm.t, 1))


def ln_rows(p, cm, ps_list, T, s_row, b_row, eps=1e-5, dim=D, wait=()):
    inv = 1.0 / dim
    nr = (T + 511) // 512
    m = cm.row()
    for r in range(nr):
        n = min(512, T - r * 512)
        p.on("vector", lambda e, r=r, n=n: e.tensor_single_scalar(
            out=m[0:1, r * 512:r * 512 + n], in_=ps_list[r][0:1, 0:n],
            scalar=inv, op=OP.mult), wait=(wait if r == 0 else ()))
        p.on("vector", lambda e, r=r, n=n: e.tensor_single_scalar(
            out=s_row[0:1, r * 512:r * 512 + n], in_=ps_list[nr + r][0:1, 0:n],
            scalar=inv, op=OP.mult))
    p.on("vector", lambda e: e.tensor_tensor(out=b_row[0:1, 0:T], in0=m[0:1, 0:T],
                                             in1=m[0:1, 0:T], op=OP.mult))
    p.on("vector", lambda e: e.tensor_tensor(out=s_row[0:1, 0:T], in0=s_row[0:1, 0:T],
                                             in1=b_row[0:1, 0:T], op=OP.subtract),
         inc=(cm.v, 1))
    p.on("vector", lambda e: e.tensor_single_scalar(out=s_row[0:1, 0:T],
                                                    in_=s_row[0:1, 0:T],
                                                    scalar=float(eps), op=OP.add),
         inc=(cm.v, 1))
    p.wait("scalar", cm.v)
    p.on("scalar", lambda e: e.activation(out=s_row[0:1, 0:T], in_=s_row[0:1, 0:T],
                                          func=AF.Ln, scale=1.0))
    p.on("scalar", lambda e: e.activation(out=s_row[0:1, 0:T], in_=s_row[0:1, 0:T],
                                          func=AF.Exp, scale=-0.5),
         inc=(cm.s, 1))
    p.wait("vector", cm.s)
    p.on("vector", lambda e: e.tensor_tensor(out=b_row[0:1, 0:T], in0=m[0:1, 0:T],
                                             in1=s_row[0:1, 0:T], op=OP.mult))
    p.on("vector", lambda e: e.tensor_single_scalar(out=b_row[0:1, 0:T],
                                                    in_=b_row[0:1, 0:T],
                                                    scalar=-1.0, op=OP.mult),
         inc=(cm.v, 1))


def bcast_rows(p, cm, row, T, ps_pair, wait=()):
    nr = (T + 511) // 512
    for r in range(nr):
        n = min(512, T - r * 512)
        p.on("tensor", lambda e, r=r, n=n: e.matmul(
            ps_pair[r][:, 0:n], cm.ones_row[:, :], row[0:1, r * 512:r * 512 + n],
            start=True, stop=True), wait=(wait if r == 0 else ()),
            inc=((cm.t, 1) if r == nr - 1 else None))


def colmax_abs(p, cm, tiles, T, scratch, out_row, ps_tp, in_bf16=False, wait=()):
    n = len(tiles)
    ntt = T // 128
    p.on("vector", lambda e: e.tensor_tensor(out=scratch[:, 0:T], in0=tiles[0][:, 0:T],
                                             in1=tiles[1][:, 0:T], op=OP.abs_max),
         wait=wait)
    for ci in range(2, n):
        p.on("vector", lambda e, ci=ci: e.tensor_tensor(
            out=scratch[:, 0:T], in0=scratch[:, 0:T], in1=tiles[ci][:, 0:T],
            op=OP.abs_max), inc=((cm.v, 1) if ci == n - 1 else None))
    if n == 2:
        p.on("vector", lambda e: e.nop(), inc=(cm.v, 1))
    ident = cm.identb if in_bf16 else cm.ident
    for tt in range(ntt):
        p.on("tensor", lambda e, tt=tt: e.transpose(
            ps_tp[:, :], scratch[:, tt * 128:(tt + 1) * 128], ident[:, :]),
            wait=([(cm.v, cm.v.c)] if tt == 0 else [(cm.v, cm.v.c)]),
            inc=(cm.t, 1))
        p.wait("vector", cm.t)
        p.on("vector", lambda e, tt=tt: e.tensor_reduce(
            out=scratch[:, T + tt:T + tt + 1], in_=ps_tp[:, :], axis=AX.X,
            op=OP.max, apply_absolute_value=True), inc=(cm.v, 1))
    p.wait("gpsimd", cm.v)
    p.on("gpsimd", lambda e: e.dma_start(
        out=cm.bounce.ap()[0:ntt * 128].rearrange("(t b) -> t b", b=ntt),
        in_=scratch[:, T:T + ntt], allow_slow_non_contiguous=True),
        inc=(cm.db, 16))
    p.wait("gpsimd", cm.db)
    p.on("gpsimd", lambda e: e.dma_start(
        out=out_row[0:1, 0:T],
        in_=cm.bounce.ap()[0:T].rearrange("t -> 1 t")), inc=(cm.db, 16))
    return cm.db.c


def colmax_to_row(p, cm, tiles, T, scratch, out_row, ps_tp, wait=()):
    ntt = T // 128
    n = len(tiles)
    fold = tiles[0]
    for tt in range(ntt):
        p.on("tensor", lambda e, tt=tt: e.transpose(
            ps_tp[:, 0:128], fold[:, tt * 128:(tt + 1) * 128], cm.ident[:, :]),
            wait=(wait if tt == 0 else [(cm.v, cm.v.c)]), inc=(cm.t, 1))
        p.wait("vector", cm.t)
        p.on("vector", lambda e, tt=tt: e.tensor_reduce(
            out=scratch[:, tt:tt + 1], in_=ps_tp[:, 0:128], axis=AX.X,
            op=OP.max, apply_absolute_value=True), inc=(cm.v, 1))
    for tt in range(ntt):
        p.on("tensor", lambda e, tt=tt: e.transpose(
            ps_tp[0:1, 0:128], scratch[:, tt:tt + 1], cm.ident[:, :]),
            wait=[(cm.v, cm.v.c)], inc=(cm.t, 1))
        p.wait("vector", cm.t)
        p.on("vector", lambda e, tt=tt: e.tensor_copy(
            out=out_row[0:1, tt * 128:(tt + 1) * 128], in_=ps_tp[0:1, 0:128]),
            inc=(cm.v, 1))


def col_to_row(p, cm, col_ap, row_out_ap, ps_tp, wait=()):
    p.on("tensor", lambda e: e.transpose(ps_tp[0:1, 0:128], col_ap, cm.ident[:, :]),
         wait=wait, inc=(cm.t, 1))
    p.wait("vector", cm.t)
    p.on("vector", lambda e: e.tensor_copy(out=row_out_ap, in_=ps_tp[0:1, 0:128]),
         inc=(cm.v, 1))


def row_to_col(p, cm, row_ap, col_out_ap, ps_tp, wait=()):
    p.on("tensor", lambda e: e.transpose(ps_tp[:, 0:128], row_ap, cm.ident[0:1, 0:128]),
         wait=wait, inc=(cm.t, 1))
    p.wait("vector", cm.t)
    p.on("vector", lambda e: e.tensor_copy(out=col_out_ap, in_=ps_tp[:, 0:1]),
         inc=(cm.v, 1))



def build_l0ffn(T=T_OWN):
    p = Prog()
    nc = p.nc
    x1_d = p.dram("x1", [D, T], F32, kind="ExternalInput").ap()
    wt_d = p.dram("WT", [D, D], F32, kind="ExternalInput").ap()
    bias_d = p.dram("bias_col", [128, NCH], F32, kind="ExternalInput").ap()
    id_d = p.dram("ident_in", [128, 128], F32, kind="ExternalInput").ap()
    x2_d = p.dram("x2", [D, T], F32, kind="ExternalOutput").ap()

    cm = Common(p, T)
    nr = T // 512

    x1 = [p.sb(f"x1_{i}", [128, T], F32) for i in range(NCH)]
    xq = [p.sb(f"xq_{i}", [128, T], BF16) for i in range(NCH)]
    outb = [p.sb(f"outb_{i}", [128, T], BF16) for i in range(NCH)]
    sqb = [p.sb(f"sq_{i}", [128, T], F32) for i in range(2)]
    rot = [p.sb(f"rot_{i}", [128, T], F32) for i in range(2)]
    csc = p.sb("csc", [128, 32], F32)
    xs_sb = p.sb("xs_sb", [128, T], F32)
    onesb = p.sb("onesb", [128, 1], BF16)
    wf = [p.sb("wf_0", [128, D], F32)]
    wabs = p.sb("wabs", [128, D], BF16)
    wt = [p.sb(f"wt_{i}", [128, D], BF16) for i in range(2)]
    fold = sqb[0]
    fmin = sqb[1]
    sqbb = [sqb[i][:, :].bitcast(BF16)[:, 0:T] for i in range(2)]
    gcol = p.sb("gcol", [128, NCH], F32)
    ginv = p.sb("ginv", [128, 1], F32)
    grow = p.sb("grow", [1, 128], F32)
    bias_c = p.sb("bias_c", [128, NCH], F32)

    P = [p.ps(f"P{i}", [128, 512], F32) for i in range(6)]
    ps_misc = p.ps("ps_misc", [128, 512], F32)
    ps_g = p.ps("ps_g", [128, 512], F32)

    cm.init_consts(identity_input_ap=id_d)
    p.on("vector", lambda e: e.memset(onesb[:, :], 1.0), inc=(cm.v, 1))
    p.wait("tensor", cm.v)
    p.wait("tensor", cm.di)

    fm_load(p, cm, x1, x1_d, T)
    p.on("sync", lambda e: e.dma_start(out=bias_c[:, :], in_=bias_d[:, :]),
         inc=(cm.dl, 16))
    x1_ready = cm.dl.c

    p.wait("tensor", cm.dl, x1_ready)
    p.wait("scalar", cm.dl, x1_ready)
    for r in range(nr):
        for ci in range(NCH):
            p.on("tensor", lambda e, r=r, ci=ci: e.matmul(
                P[r][0:1, :], cm.ones_col[:, :], x1[ci][:, r * 512:(r + 1) * 512],
                start=(ci == 0), stop=(ci == NCH - 1)),
                inc=((cm.t, 1) if ci == NCH - 1 else None))
    mm_done = {}
    for ci in range(NCH):
        w = [(cm.t, mm_done[ci - 2])] if ci >= 2 else ()
        p.on("scalar", lambda e, ci=ci: e.activation(
            out=sqb[ci % 2][:, :], in_=x1[ci][:, :], func=AF.Square),
            wait=w, inc=(cm.s, 1))
        p.wait("tensor", cm.s)
        for r in range(nr):
            p.on("tensor", lambda e, r=r, ci=ci: e.matmul(
                P[nr + r][0:1, :], cm.ones_col[:, :],
                sqb[ci % 2][:, r * 512:(r + 1) * 512],
                start=(ci == 0), stop=(ci == NCH - 1)),
                inc=(cm.t, 1))
        mm_done[ci] = cm.t.c

    s1 = cm.row()
    b1 = cm.row()
    ln_rows(p, cm, P, T, s1, b1, eps=1e-5, dim=D, wait=[(cm.t, cm.t.c)])
    r2 = cm.row()
    p.on("vector", lambda e: e.tensor_tensor(out=r2[0:1, 0:T], in0=s1[0:1, 0:T],
                                             in1=s1[0:1, 0:T], op=OP.mult))
    p.on("vector", lambda e: e.tensor_single_scalar(out=r2[0:1, 0:T], in_=r2[0:1, 0:T],
                                                    scalar=-1e-5, op=OP.mult))
    p.on("vector", lambda e: e.tensor_single_scalar(out=r2[0:1, 0:T], in_=r2[0:1, 0:T],
                                                    scalar=1.0 + 1e-8, op=OP.add),
         inc=(cm.v, 1))
    p.wait("scalar", cm.v)
    p.on("scalar", lambda e: e.activation(out=r2[0:1, 0:T], in_=r2[0:1, 0:T],
                                          func=AF.Ln, scale=1.0))
    p.on("scalar", lambda e: e.activation(out=r2[0:1, 0:T], in_=r2[0:1, 0:T],
                                          func=AF.Exp, scale=-0.5),
         inc=(cm.s, 1))
    p.wait("vector", cm.s)
    p.on("vector", lambda e: e.tensor_tensor(out=s1[0:1, 0:T], in0=s1[0:1, 0:T],
                                             in1=r2[0:1, 0:T], op=OP.mult))
    p.on("vector", lambda e: e.tensor_tensor(out=b1[0:1, 0:T], in0=b1[0:1, 0:T],
                                             in1=r2[0:1, 0:T], op=OP.mult),
         inc=(cm.v, 1))

    bcast_rows(p, cm, s1, T, [P[0], P[1]], wait=[(cm.v, cm.v.c)])
    bcast_rows(p, cm, b1, T, [P[2], P[3]])

    p.wait("vector", cm.t)
    for ci in range(NCH):
        rb = rot[ci % 2]
        for r in range(nr):
            p.on("vector", lambda e, rb=rb, ci=ci, r=r: e.tensor_tensor(
                out=rb[:, r * 512:(r + 1) * 512], in0=x1[ci][:, r * 512:(r + 1) * 512],
                in1=P[r][:, :], op=OP.mult))
            p.on("vector", lambda e, rb=rb, ci=ci, r=r: e.tensor_tensor(
                out=rb[:, r * 512:(r + 1) * 512], in0=rb[:, r * 512:(r + 1) * 512],
                in1=P[2 + r][:, :], op=OP.add))
        if ci == 0:
            p.on("vector", lambda e: e.tensor_copy(out=fold[:, :], in_=rot[0][:, :]))
            p.on("vector", lambda e: e.tensor_copy(out=fmin[:, :], in_=rot[0][:, :]))
        else:
            p.on("vector", lambda e, rb=rb: e.tensor_tensor(
                out=fold[:, :], in0=fold[:, :], in1=rb[:, :], op=OP.max))
            p.on("vector", lambda e, rb=rb: e.tensor_tensor(
                out=fmin[:, :], in0=fmin[:, :], in1=rb[:, :], op=OP.min),
                inc=((cm.v, 1) if ci == NCH - 1 else None))
    p.on("vector", lambda e: e.tensor_single_scalar(out=fmin[:, :], in_=fmin[:, :],
                                                    scalar=-1.0, op=OP.mult))
    p.on("vector", lambda e: e.tensor_tensor(out=fold[:, :], in0=fold[:, :],
                                             in1=fmin[:, :], op=OP.max),
         inc=(cm.v, 1))

    xsr = cm.row_at(4)
    colmax_to_row(p, cm, [fold], T, csc, xsr, ps_misc, wait=[(cm.v, cm.v.c)])
    inv = cm.row_at(5)
    p.on("vector", lambda e: e.tensor_single_scalar(out=xsr[0:1, 0:T], in_=xsr[0:1, 0:T],
                                                    scalar=1.0 / 127.0, op=OP.mult))
    p.on("vector", lambda e: e.tensor_single_scalar(out=inv[0:1, 0:T], in_=xsr[0:1, 0:T],
                                                    scalar=1e-8, op=OP.add))
    p.on("vector", lambda e: e.reciprocal(out=inv[0:1, 0:T], in_=inv[0:1, 0:T]))
    p.on("vector", lambda e: e.tensor_tensor(out=s1[0:1, 0:T], in0=s1[0:1, 0:T],
                                             in1=inv[0:1, 0:T], op=OP.mult))
    p.on("vector", lambda e: e.tensor_tensor(out=b1[0:1, 0:T], in0=b1[0:1, 0:T],
                                             in1=inv[0:1, 0:T], op=OP.mult),
         inc=(cm.v, 1))
    bcast_rows(p, cm, s1, T, [P[0], P[1]], wait=[(cm.v, cm.v.c)])
    bcast_rows(p, cm, b1, T, [P[2], P[3]])
    bcast_rows(p, cm, xsr, T, [P[4], P[5]])
    p.wait("vector", cm.t)
    for r in range(nr):
        p.on("vector", lambda e, r=r: e.tensor_copy(
            out=xs_sb[:, r * 512:(r + 1) * 512], in_=P[4 + r][:, :]), inc=(cm.v, 1))

    for ci in range(NCH):
        rb = rot[ci % 2]
        for r in range(nr):
            p.on("vector", lambda e, rb=rb, ci=ci, r=r: e.tensor_tensor(
                out=rb[:, r * 512:(r + 1) * 512], in0=x1[ci][:, r * 512:(r + 1) * 512],
                in1=P[r][:, :], op=OP.mult))
            p.on("vector", lambda e, rb=rb, ci=ci, r=r: e.tensor_tensor(
                out=rb[:, r * 512:(r + 1) * 512], in0=rb[:, r * 512:(r + 1) * 512],
                in1=P[2 + r][:, :], op=OP.add))
        p.on("vector", lambda e, rb=rb: e.tensor_single_scalar(
            out=rb[:, :], in_=rb[:, :], scalar=12582912.0, op=OP.add))
        p.on("vector", lambda e, rb=rb: e.tensor_single_scalar(
            out=rb[:, :], in_=rb[:, :], scalar=-12582912.0, op=OP.add))
        p.on("vector", lambda e, ci=ci, rb=rb: e.tensor_copy(out=xq[ci][:, :], in_=rb[:, :]),
             inc=(cm.v, 1))
    quant_done = cm.v.c

    p.wait("gpsimd", cm.v, quant_done)
    wld = {}
    for g in range(NCH):
        wb = wf[0]
        wget = [(cm.v, wld[g - 1])] if g >= 1 else ()
        for ci in range(NCH):
            p.on("gpsimd", lambda e, wb=wb, g=g, ci=ci: e.dma_start(
                out=wb[:, ci * 128:(ci + 1) * 128],
                in_=wt_d[ci * 128:(ci + 1) * 128, g * 128:(g + 1) * 128]),
                wait=(wget if ci == 0 else ()), inc=(cm.dw, 16))
        ld_tgt = cm.dw.c
        p.wait("scalar", cm.dw, ld_tgt)
        p.on("scalar", lambda e, wb=wb: e.activation(out=wabs[:, :], in_=wb[:, :],
                                                     func=AF.Abs), inc=(cm.s, 1))
        p.wait("tensor", cm.s)
        for ci in range(NCH):
            p.on("tensor", lambda e, g=g, ci=ci: e.matmul(
                ps_g[:, g:g + 1], wabs[:, ci * 128:(ci + 1) * 128], onesb[:, :],
                start=(ci == 0), stop=(ci == NCH - 1)),
                inc=((cm.t, 1) if ci == NCH - 1 else None))
        p.wait("vector", cm.t)
        p.on("vector", lambda e, g=g: e.tensor_single_scalar(
            out=gcol[:, g:g + 1], in_=ps_g[:, g:g + 1], scalar=1.0 / D, op=OP.mult))
        p.on("vector", lambda e, g=g: e.tensor_single_scalar(
            out=ginv[:, 0:1], in_=gcol[:, g:g + 1], scalar=1e-8, op=OP.add))
        p.on("vector", lambda e: e.reciprocal(out=ginv[:, 0:1], in_=ginv[:, 0:1]),
             inc=(cm.v, 1))
        col_to_row(p, cm, ginv[:, 0:1], grow[0:1, :], ps_misc, wait=[(cm.v, cm.v.c)])
        p.on("tensor", lambda e: e.matmul(ps_misc[:, 0:128], cm.ones_row[:, :],
                                          grow[0:1, :], start=True, stop=True),
             wait=[(cm.v, cm.v.c)], inc=(cm.t, 1))
        p.wait("vector", cm.t)
        wbv = wb[:, :].rearrange("p (c f) -> p c f", f=128)
        p.on("vector", lambda e, wb=wb, wbv=wbv: e.tensor_tensor(
            out=wbv, in0=wbv,
            in1=ps_misc[:, 0:128].unsqueeze(1).broadcast_to([128, NCH, 128]), op=OP.mult))
        p.on("vector", lambda e, wb=wb: e.tensor_scalar(
            out=wb[:, :], in0=wb[:, :], scalar1=1.4999, scalar2=-1.4999,
            op0=OP.min, op1=OP.max))
        p.on("vector", lambda e, wb=wb: e.tensor_single_scalar(
            out=wb[:, :], in_=wb[:, :], scalar=12582912.0, op=OP.add))
        p.on("vector", lambda e, wb=wb: e.tensor_single_scalar(
            out=wb[:, :], in_=wb[:, :], scalar=-12582912.0, op=OP.add))
        p.on("vector", lambda e, g=g, wb=wb: e.tensor_copy(out=wt[g % 2][:, :], in_=wb[:, :]),
             inc=(cm.v, 1))
        wld[g] = cm.v.c
        tern_done = cm.v.c
        for r in range(nr):
            ps_o = P[4 + (g % 2)] if nr == 1 else P[4 + r]
            p.wait("tensor", cm.v, tern_done)
            for ci in range(NCH):
                p.on("tensor", lambda e, ps_o=ps_o, g=g, ci=ci, r=r: e.matmul(
                    ps_o[:, :], wt[g % 2][:, ci * 128:(ci + 1) * 128],
                    xq[ci][:, r * 512:(r + 1) * 512],
                    start=(ci == 0), stop=(ci == NCH - 1)),
                    inc=((cm.t, 1) if ci == NCH - 1 else None))
            mm_tgt = cm.t.c
            p.wait("vector", cm.t, mm_tgt)
            p.on("vector", lambda e, ps_o=ps_o, r=r: e.tensor_tensor(
                out=sqb[r][:, 0:512], in0=ps_o[:, :],
                in1=xs_sb[:, r * 512:(r + 1) * 512], op=OP.mult), inc=(cm.v, 1))
            p.wait("scalar", cm.v)
            p.on("scalar", lambda e, g=g, r=r: e.activation(
                out=outb[g][:, r * 512:(r + 1) * 512], in_=sqb[r][:, 0:512],
                func=AF.Identity, bias=bias_c[:, g:g + 1], scale=gcol[:, g:g + 1]),
                inc=(cm.s, 1))
    p.barrier()

    for r in range(nr):
        for ci in range(NCH):
            p.on("tensor", lambda e, r=r, ci=ci: e.matmul(
                P[r][0:1, :], onesb[:, :], outb[ci][:, r * 512:(r + 1) * 512],
                start=(ci == 0), stop=(ci == NCH - 1)),
                inc=((cm.t, 1) if ci == NCH - 1 else None))
    mm_done = {}
    for ci in range(NCH):
        w = [(cm.t, mm_done[ci - 2])] if ci >= 2 else ()
        p.on("scalar", lambda e, ci=ci: e.activation(
            out=sqbb[ci % 2], in_=outb[ci][:, :], func=AF.Square),
            wait=w, inc=(cm.s, 1))
        p.wait("tensor", cm.s)
        for r in range(nr):
            p.on("tensor", lambda e, r=r, ci=ci: e.matmul(
                P[nr + r][0:1, :], onesb[:, :], sqbb[ci % 2][:, r * 512:(r + 1) * 512],
                start=(ci == 0), stop=(ci == NCH - 1)),
                inc=(cm.t, 1))
        mm_done[ci] = cm.t.c
    cm._rows_used = 0
    s3 = cm.row()
    b3 = cm.row()
    ln_rows(p, cm, P, T, s3, b3, eps=1e-5, dim=D, wait=[(cm.t, cm.t.c)])
    bcast_rows(p, cm, s3, T, [P[0], P[1]], wait=[(cm.v, cm.v.c)])
    bcast_rows(p, cm, b3, T, [P[2], P[3]])
    p.wait("vector", cm.t)

    st_sems = [p.sem("st0"), p.sem("st1")]
    st_done = {}
    for ci in range(NCH):
        rb = rot[ci % 2]
        w = [(st_sems[ci % 2], st_done[ci - 2])] if ci >= 2 else ()
        for r in range(nr):
            p.on("vector", lambda e, rb=rb, ci=ci, r=r: e.tensor_tensor(
                out=rb[:, r * 512:(r + 1) * 512], in0=outb[ci][:, r * 512:(r + 1) * 512],
                in1=P[r][:, :], op=OP.mult), wait=(w if r == 0 else ()))
            p.on("vector", lambda e, rb=rb, ci=ci, r=r: e.tensor_tensor(
                out=rb[:, r * 512:(r + 1) * 512], in0=rb[:, r * 512:(r + 1) * 512],
                in1=P[2 + r][:, :], op=OP.add))
        p.on("vector", lambda e, rb=rb, ci=ci: e.tensor_tensor(
            out=rb[:, :], in0=rb[:, :], in1=x1[ci][:, :], op=OP.add), inc=(cm.v, 1))
        p.wait("sync", cm.v)
        p.on("sync", lambda e, rb=rb, ci=ci: e.dma_start(
            out=x2_d[ci * 128:(ci + 1) * 128, 0:T], in_=rb[:, :]),
            inc=(st_sems[ci % 2], 16))
        st_done[ci] = st_sems[ci % 2].c
    p.wait("sync", st_sems[0])
    p.wait("sync", st_sems[1])
    p.wait("gpsimd", st_sems[0])
    p.wait("gpsimd", st_sems[1])
    return p.build()


def host_l0ffn_inputs(x1_fm, weight, bias):
    return {
        "x1": np.ascontiguousarray(x1_fm, np.float32),
        "WT": np.ascontiguousarray(weight.T, np.float32),
        "bias_col": np.ascontiguousarray(bias.reshape(NCH, 128).T, np.float32),
        "ident_in": np.eye(128, dtype=np.float32),
    }




def colmax_to_row_bf(p, cm, fold_bf, T, scratch, out_row, ps_tp, wait=()):
    ntt = T // 128
    psb = ps_tp[:, 0:128].bitcast(BF16)[:, 0:128]
    for tt in range(ntt):
        p.on("tensor", lambda e, tt=tt: e.transpose(
            psb, fold_bf[:, tt * 128:(tt + 1) * 128], cm.identb[:, :]),
            wait=(wait if tt == 0 else [(cm.v, cm.v.c)]), inc=(cm.t, 1))
        p.wait("vector", cm.t)
        p.on("vector", lambda e, tt=tt: e.tensor_reduce(
            out=scratch[:, tt:tt + 1], in_=psb, axis=AX.X,
            op=OP.max, apply_absolute_value=True), inc=(cm.v, 1))
    for tt in range(ntt):
        p.on("tensor", lambda e, tt=tt: e.transpose(
            ps_tp[0:1, 0:128], scratch[:, tt:tt + 1], cm.ident[:, :]),
            wait=[(cm.v, cm.v.c)], inc=(cm.t, 1))
        p.wait("vector", cm.t)
        p.on("vector", lambda e, tt=tt: e.tensor_copy(
            out=out_row[0:1, tt * 128:(tt + 1) * 128], in_=ps_tp[0:1, 0:128]),
            inc=(cm.v, 1))



def build_l1ffn(T=T_OWN):
    p = Prog()
    x3_d = p.dram("x3", [D, T], F32, kind="ExternalInput").ap()
    wt_d = p.dram("WT", [D, D], F32, kind="ExternalInput").ap()
    bias_d = p.dram("bias_col", [128, NCH], F32, kind="ExternalInput").ap()
    id_d = p.dram("ident_in", [128, 128], F32, kind="ExternalInput").ap()
    x4_d = p.dram("x4", [D, T], F32, kind="ExternalOutput").ap()

    cm = Common(p, T)
    nr = T // 512
    MG = 12582912.0

    x3 = [p.sb(f"x3_{i}", [128, T], F32) for i in range(NCH)]
    xq = [p.sb(f"xq_{i}", [128, T], BF16) for i in range(NCH)]
    yv = [p.sb(f"y_{i}", [128, T], BF16) for i in range(NCH)]
    yq = xq
    sqb = [p.sb(f"sq_{i}", [128, T], F32) for i in range(2)]
    rot = [p.sb(f"rot_{i}", [128, T], F32) for i in range(2)]
    fold = sqb[0]
    fmin = sqb[1]
    sqbb = [sqb[i][:, :].bitcast(BF16)[:, 0:T] for i in range(2)]
    csc = p.sb("csc", [128, 32], F32)
    xs_sb = p.sb("xs_sb", [128, T], F32)
    onesb = p.sb("onesb", [128, 1], BF16)
    wf = [p.sb("wf_0", [128, D], F32)]
    wt = [p.sb("wtq", [128, D], BF16)]
    skcol = p.sb("skcol", [128, NCH], F32)
    kinv = p.sb("kinv", [128, 1], F32)
    krow = p.sb("krow", [1, 128], F32)
    bias_c = p.sb("bias_c", [128, NCH], F32)
    bq_c = p.sb("bq_c", [128, NCH], F32)
    sb_bc = p.sb("sb_bc", [128, 1], F32)

    P = [p.ps(f"P{i}", [128, 512], F32) for i in range(6)]
    ps_misc = p.ps("ps_misc", [128, 512], F32)
    ps_g = p.ps("ps_g", [128, 512], F32)

    cm.init_consts(identity_input_ap=id_d)
    p.on("vector", lambda e: e.memset(onesb[:, :], 1.0))
    p.wait("tensor", cm.v)
    p.wait("tensor", cm.di)

    fm_load(p, cm, x3, x3_d, T)
    p.on("sync", lambda e: e.dma_start(out=bias_c[:, :], in_=bias_d[:, :]),
         inc=(cm.dl, 16))
    x_ready = cm.dl.c

    p.wait("vector", cm.dl, x_ready)
    p.on("vector", lambda e: e.tensor_reduce(out=csc[:, 0:1], in_=bias_c[:, :],
                                             axis=AX.X, op=OP.max,
                                             apply_absolute_value=True))
    col_to_row(p, cm, csc[:, 0:1], krow[0:1, :], ps_misc, wait=[(cm.v, cm.v.c)])
    p.on("vector", lambda e: e.tensor_reduce(out=krow[0:1, 0:1], in_=krow[0:1, :],
                                             axis=AX.X, op=OP.max,
                                             apply_absolute_value=True))
    p.on("vector", lambda e: e.tensor_single_scalar(out=krow[0:1, 0:1],
                                                    in_=krow[0:1, 0:1],
                                                    scalar=1.0 / 127.0, op=OP.mult),
         inc=(cm.v, 1))
    p.on("tensor", lambda e: e.matmul(ps_misc[:, 0:1], cm.ones_row[:, :],
                                      krow[0:1, 0:1], start=True, stop=True),
         wait=[(cm.v, cm.v.c)], inc=(cm.t, 1))
    p.wait("vector", cm.t)
    p.on("vector", lambda e: e.tensor_copy(out=sb_bc[:, 0:1], in_=ps_misc[:, 0:1]))
    p.on("vector", lambda e: e.reciprocal(out=csc[:, 1:2], in_=sb_bc[:, 0:1]))
    p.on("vector", lambda e: e.tensor_tensor(
        out=bq_c[:, :], in0=bias_c[:, :],
        in1=csc[:, 1:2].broadcast_to([128, NCH]), op=OP.mult))
    p.on("vector", lambda e: e.tensor_single_scalar(out=bq_c[:, :], in_=bq_c[:, :],
                                                    scalar=MG, op=OP.add))
    p.on("vector", lambda e: e.tensor_single_scalar(out=bq_c[:, :], in_=bq_c[:, :],
                                                    scalar=-MG, op=OP.add))
    p.on("vector", lambda e: e.tensor_tensor(
        out=bq_c[:, :], in0=bq_c[:, :], in1=sb_bc[:, 0:1].broadcast_to([128, NCH]),
        op=OP.mult), inc=(cm.v, 1))

    p.wait("tensor", cm.dl, x_ready)
    p.wait("scalar", cm.dl, x_ready)
    for r in range(nr):
        for ci in range(NCH):
            p.on("tensor", lambda e, r=r, ci=ci: e.matmul(
                P[r][0:1, :], cm.ones_col[:, :], x3[ci][:, r * 512:(r + 1) * 512],
                start=(ci == 0), stop=(ci == NCH - 1)),
                inc=((cm.t, 1) if ci == NCH - 1 else None))
    mm_done = {}
    for ci in range(NCH):
        w = [(cm.t, mm_done[ci - 2])] if ci >= 2 else ()
        p.on("scalar", lambda e, ci=ci: e.activation(
            out=sqb[ci % 2][:, :], in_=x3[ci][:, :], func=AF.Square),
            wait=w, inc=(cm.s, 1))
        p.wait("tensor", cm.s)
        for r in range(nr):
            p.on("tensor", lambda e, r=r, ci=ci: e.matmul(
                P[nr + r][0:1, :], cm.ones_col[:, :],
                sqb[ci % 2][:, r * 512:(r + 1) * 512],
                start=(ci == 0), stop=(ci == NCH - 1)), inc=(cm.t, 1))
        mm_done[ci] = cm.t.c

    s1 = cm.row()
    b1 = cm.row()
    ln_rows(p, cm, P, T, s1, b1, eps=1e-5, dim=D, wait=[(cm.t, cm.t.c)])

    bcast_rows(p, cm, s1, T, [P[0], P[1]], wait=[(cm.v, cm.v.c)])
    bcast_rows(p, cm, b1, T, [P[2], P[3]])
    p.wait("vector", cm.t)
    for ci in range(NCH):
        rb = rot[ci % 2]
        for r in range(nr):
            p.on("vector", lambda e, rb=rb, ci=ci, r=r: e.tensor_tensor(
                out=rb[:, r * 512:(r + 1) * 512], in0=x3[ci][:, r * 512:(r + 1) * 512],
                in1=P[r][:, :], op=OP.mult))
            p.on("vector", lambda e, rb=rb, ci=ci, r=r: e.tensor_tensor(
                out=rb[:, r * 512:(r + 1) * 512], in0=rb[:, r * 512:(r + 1) * 512],
                in1=P[2 + r][:, :], op=OP.add))
        if ci == 0:
            p.on("vector", lambda e: e.tensor_copy(out=fold[:, :], in_=rot[0][:, :]))
            p.on("vector", lambda e: e.tensor_copy(out=fmin[:, :], in_=rot[0][:, :]))
        else:
            p.on("vector", lambda e, rb=rb: e.tensor_tensor(
                out=fold[:, :], in0=fold[:, :], in1=rb[:, :], op=OP.max))
            p.on("vector", lambda e, rb=rb: e.tensor_tensor(
                out=fmin[:, :], in0=fmin[:, :], in1=rb[:, :], op=OP.min))
    p.on("vector", lambda e: e.tensor_single_scalar(out=fmin[:, :], in_=fmin[:, :],
                                                    scalar=-1.0, op=OP.mult))
    p.on("vector", lambda e: e.tensor_tensor(out=fold[:, :], in0=fold[:, :],
                                             in1=fmin[:, :], op=OP.max), inc=(cm.v, 1))

    xsr = cm.row_at(4)
    colmax_to_row(p, cm, [fold], T, csc[:, 2:30], xsr, ps_misc, wait=[(cm.v, cm.v.c)])
    inv = cm.row_at(5)
    p.on("vector", lambda e: e.tensor_single_scalar(out=xsr[0:1, 0:T], in_=xsr[0:1, 0:T],
                                                    scalar=1.0 / 127.0, op=OP.mult))
    p.on("vector", lambda e: e.reciprocal(out=inv[0:1, 0:T], in_=xsr[0:1, 0:T]))
    p.on("vector", lambda e: e.tensor_tensor(out=s1[0:1, 0:T], in0=s1[0:1, 0:T],
                                             in1=inv[0:1, 0:T], op=OP.mult))
    p.on("vector", lambda e: e.tensor_tensor(out=b1[0:1, 0:T], in0=b1[0:1, 0:T],
                                             in1=inv[0:1, 0:T], op=OP.mult),
         inc=(cm.v, 1))
    bcast_rows(p, cm, s1, T, [P[0], P[1]], wait=[(cm.v, cm.v.c)])
    bcast_rows(p, cm, b1, T, [P[2], P[3]])
    bcast_rows(p, cm, xsr, T, [P[4], P[5]])
    p.wait("vector", cm.t)
    for r in range(nr):
        p.on("vector", lambda e, r=r: e.tensor_copy(
            out=xs_sb[:, r * 512:(r + 1) * 512], in_=P[4 + r][:, :]))

    for ci in range(NCH):
        rb = rot[ci % 2]
        for r in range(nr):
            p.on("vector", lambda e, rb=rb, ci=ci, r=r: e.tensor_tensor(
                out=rb[:, r * 512:(r + 1) * 512], in0=x3[ci][:, r * 512:(r + 1) * 512],
                in1=P[r][:, :], op=OP.mult))
            p.on("vector", lambda e, rb=rb, ci=ci, r=r: e.tensor_tensor(
                out=rb[:, r * 512:(r + 1) * 512], in0=rb[:, r * 512:(r + 1) * 512],
                in1=P[2 + r][:, :], op=OP.add))
        p.on("vector", lambda e, rb=rb: e.tensor_single_scalar(
            out=rb[:, :], in_=rb[:, :], scalar=MG, op=OP.add))
        p.on("vector", lambda e, rb=rb: e.tensor_single_scalar(
            out=rb[:, :], in_=rb[:, :], scalar=-MG, op=OP.add))
        p.on("vector", lambda e, ci=ci, rb=rb: e.tensor_copy(out=xq[ci][:, :],
                                                             in_=rb[:, :]),
             inc=(cm.v, 1))
    quant_done = cm.v.c

    p.wait("gpsimd", cm.v, quant_done)
    wld = {}
    last_mm = [cm.t.c]
    for g in range(NCH):
        wb = wf[0]
        wget = [(cm.v, wld[g - 1])] if g >= 1 else ()
        for ci in range(NCH):
            p.on("gpsimd", lambda e, wb=wb, g=g, ci=ci: e.dma_start(
                out=wb[:, ci * 128:(ci + 1) * 128],
                in_=wt_d[ci * 128:(ci + 1) * 128, g * 128:(g + 1) * 128]),
                wait=(wget if ci == 0 else ()), inc=(cm.dw, 16))
        ld_tgt = cm.dw.c
        p.wait("vector", cm.dw, ld_tgt)
        wv = wb[:, :].rearrange("p (c f) -> p c f", f=128)
        fmx, fmn = rot[0], rot[1]
        p.on("vector", lambda e: e.tensor_copy(out=fmx[:, 0:128], in_=wb[:, 0:128]))
        p.on("vector", lambda e: e.tensor_copy(out=fmn[:, 0:128], in_=wb[:, 0:128]))
        for ci in range(1, NCH):
            p.on("vector", lambda e, ci=ci: e.tensor_tensor(
                out=fmx[:, 0:128], in0=fmx[:, 0:128],
                in1=wb[:, ci * 128:(ci + 1) * 128], op=OP.max))
            p.on("vector", lambda e, ci=ci: e.tensor_tensor(
                out=fmn[:, 0:128], in0=fmn[:, 0:128],
                in1=wb[:, ci * 128:(ci + 1) * 128], op=OP.min))
        p.on("vector", lambda e: e.tensor_single_scalar(
            out=fmn[:, 0:128], in_=fmn[:, 0:128], scalar=-1.0, op=OP.mult))
        p.on("vector", lambda e: e.tensor_tensor(
            out=fmx[:, 0:128], in0=fmx[:, 0:128], in1=fmn[:, 0:128], op=OP.max),
            inc=(cm.v, 1))
        p.on("tensor", lambda e: e.transpose(ps_misc[:, 0:128], fmx[:, 0:128],
                                             cm.ident[:, :]),
             wait=[(cm.v, cm.v.c)], inc=(cm.t, 1))
        p.wait("vector", cm.t)
        p.on("vector", lambda e, g=g: e.tensor_reduce(
            out=skcol[:, g:g + 1], in_=ps_misc[:, 0:128], axis=AX.X, op=OP.max,
            apply_absolute_value=True))
        p.on("vector", lambda e, g=g: e.tensor_single_scalar(
            out=skcol[:, g:g + 1], in_=skcol[:, g:g + 1], scalar=1.0 / 127.0,
            op=OP.mult))
        p.on("vector", lambda e, g=g: e.reciprocal(out=kinv[:, 0:1],
                                                   in_=skcol[:, g:g + 1]),
             inc=(cm.v, 1))
        col_to_row(p, cm, kinv[:, 0:1], krow[0:1, :], ps_misc, wait=[(cm.v, cm.v.c)])
        p.on("tensor", lambda e: e.matmul(ps_misc[:, 0:128], cm.ones_row[:, :],
                                          krow[0:1, :], start=True, stop=True),
             wait=[(cm.v, cm.v.c)], inc=(cm.t, 1))
        p.wait("vector", cm.t)
        p.on("vector", lambda e, wv=wv: e.tensor_tensor(
            out=wv, in0=wv,
            in1=ps_misc[:, 0:128].unsqueeze(1).broadcast_to([128, NCH, 128]),
            op=OP.mult))
        p.on("vector", lambda e, wb=wb: e.tensor_single_scalar(
            out=wb[:, :], in_=wb[:, :], scalar=MG, op=OP.add))
        p.on("vector", lambda e, wb=wb: e.tensor_single_scalar(
            out=wb[:, :], in_=wb[:, :], scalar=-MG, op=OP.add))
        p.on("vector", lambda e, g=g, wb=wb: e.tensor_copy(out=wt[0][:, :],
                                                           in_=wb[:, :]),
             wait=[(cm.t, last_mm[0])], inc=(cm.v, 1))
        wld[g] = cm.v.c
        tern_done = cm.v.c
        for r in range(nr):
            ps_o = P[4 + r]
            p.wait("tensor", cm.v, tern_done)
            for ci in range(NCH):
                p.on("tensor", lambda e, ps_o=ps_o, g=g, ci=ci, r=r: e.matmul(
                    ps_o[:, :], wt[0][:, ci * 128:(ci + 1) * 128],
                    xq[ci][:, r * 512:(r + 1) * 512],
                    start=(ci == 0), stop=(ci == NCH - 1)),
                    inc=((cm.t, 1) if ci == NCH - 1 else None))
            mm_tgt = cm.t.c
            last_mm[0] = mm_tgt
            p.wait("vector", cm.t, mm_tgt)
            p.on("vector", lambda e, ps_o=ps_o, g=g, r=r: e.tensor_tensor(
                out=yv[g][:, r * 512:(r + 1) * 512], in0=ps_o[:, :],
                in1=xs_sb[:, r * 512:(r + 1) * 512], op=OP.mult), inc=(cm.v, 1))
            p.wait("scalar", cm.v)
            p.on("scalar", lambda e, g=g, r=r: e.activation(
                out=yv[g][:, r * 512:(r + 1) * 512], in_=yv[g][:, r * 512:(r + 1) * 512],
                func=AF.Relu, bias=bq_c[:, g:g + 1], scale=skcol[:, g:g + 1]),
                inc=(cm.s, 1))
    p.barrier()

    foldb = sqbb[0]
    for ci in range(NCH):
        if ci == 0:
            p.on("vector", lambda e, foldb=foldb: e.tensor_copy(out=foldb, in_=yv[0][:, :]))
        else:
            p.on("vector", lambda e, ci=ci, foldb=foldb: e.tensor_tensor(
                out=foldb, in0=foldb, in1=yv[ci][:, :], op=OP.max),
                inc=((cm.v, 1) if ci == NCH - 1 else None))
    syr = cm.row_at(4)
    colmax_to_row_bf(p, cm, foldb, T, csc[:, 2:30], syr, ps_misc, wait=[(cm.v, cm.v.c)])
    syi = cm.row_at(5)
    p.on("vector", lambda e: e.tensor_single_scalar(out=syr[0:1, 0:T], in_=syr[0:1, 0:T],
                                                    scalar=1.0 / 127.0, op=OP.mult))
    p.on("vector", lambda e: e.tensor_single_scalar(out=syi[0:1, 0:T], in_=syr[0:1, 0:T],
                                                    scalar=1e-30, op=OP.add))
    p.on("vector", lambda e: e.reciprocal(out=syi[0:1, 0:T], in_=syi[0:1, 0:T]),
         inc=(cm.v, 1))
    bcast_rows(p, cm, syi, T, [P[0], P[1]], wait=[(cm.v, cm.v.c)])
    p.wait("vector", cm.t)
    for ci in range(NCH):
        rb = rot[ci % 2]
        p.on("vector", lambda e, rb=rb, ci=ci: e.tensor_copy(out=rb[:, :],
                                                             in_=yv[ci][:, :]))
        for r in range(nr):
            p.on("vector", lambda e, rb=rb, ci=ci, r=r: e.tensor_tensor(
                out=rb[:, r * 512:(r + 1) * 512], in0=rb[:, r * 512:(r + 1) * 512],
                in1=P[r][:, :], op=OP.mult))
        p.on("vector", lambda e, rb=rb: e.tensor_single_scalar(
            out=rb[:, :], in_=rb[:, :], scalar=MG, op=OP.add))
        p.on("vector", lambda e, rb=rb: e.tensor_single_scalar(
            out=rb[:, :], in_=rb[:, :], scalar=-MG, op=OP.add))
        p.on("vector", lambda e, ci=ci, rb=rb: e.tensor_copy(out=yq[ci][:, :],
                                                             in_=rb[:, :]),
             inc=(cm.v, 1))
    p.barrier()

    for r in range(nr):
        for ci in range(NCH):
            p.on("tensor", lambda e, r=r, ci=ci: e.matmul(
                P[r][0:1, :], onesb[:, :], yq[ci][:, r * 512:(r + 1) * 512],
                start=(ci == 0), stop=(ci == NCH - 1)),
                inc=((cm.t, 1) if ci == NCH - 1 else None))
    mm_done = {}
    for ci in range(NCH):
        w = [(cm.t, mm_done[ci - 2])] if ci >= 2 else ()
        p.on("scalar", lambda e, ci=ci: e.activation(
            out=sqbb[ci % 2], in_=yq[ci][:, :], func=AF.Square),
            wait=w, inc=(cm.s, 1))
        p.wait("tensor", cm.s)
        for r in range(nr):
            p.on("tensor", lambda e, r=r, ci=ci: e.matmul(
                P[nr + r][0:1, :], onesb[:, :], sqbb[ci % 2][:, r * 512:(r + 1) * 512],
                start=(ci == 0), stop=(ci == NCH - 1)), inc=(cm.t, 1))
        mm_done[ci] = cm.t.c
    cm._rows_used = 0
    s3 = cm.row()
    b3 = cm.row()
    ln_rows(p, cm, P, T, s3, b3, eps=1e-5, dim=D, wait=[(cm.t, cm.t.c)])
    bcast_rows(p, cm, s3, T, [P[0], P[1]], wait=[(cm.v, cm.v.c)])
    bcast_rows(p, cm, b3, T, [P[2], P[3]])
    p.wait("vector", cm.t)

    st_sems = [p.sem("st0"), p.sem("st1")]
    st_done = {}
    for ci in range(NCH):
        rb = rot[ci % 2]
        w = [(st_sems[ci % 2], st_done[ci - 2])] if ci >= 2 else ()
        for r in range(nr):
            p.on("vector", lambda e, rb=rb, ci=ci, r=r: e.tensor_tensor(
                out=rb[:, r * 512:(r + 1) * 512], in0=yq[ci][:, r * 512:(r + 1) * 512],
                in1=P[r][:, :], op=OP.mult), wait=(w if r == 0 else ()))
            p.on("vector", lambda e, rb=rb, ci=ci, r=r: e.tensor_tensor(
                out=rb[:, r * 512:(r + 1) * 512], in0=rb[:, r * 512:(r + 1) * 512],
                in1=P[2 + r][:, :], op=OP.add))
        p.on("vector", lambda e, rb=rb, ci=ci: e.tensor_tensor(
            out=rb[:, :], in0=rb[:, :], in1=x3[ci][:, :], op=OP.add), inc=(cm.v, 1))
        p.wait("sync", cm.v)
        p.on("sync", lambda e, rb=rb, ci=ci: e.dma_start(
            out=x4_d[ci * 128:(ci + 1) * 128, 0:T], in_=rb[:, :]),
            inc=(st_sems[ci % 2], 16))
        st_done[ci] = st_sems[ci % 2].c
    p.wait("sync", st_sems[0])
    p.wait("sync", st_sems[1])
    p.wait("gpsimd", st_sems[0])
    p.wait("gpsimd", st_sems[1])
    return p.build()


def host_l1ffn_inputs(x3_fm, kernel_w, bias):
    return {
        "x3": np.ascontiguousarray(x3_fm, np.float32),
        "WT": np.ascontiguousarray(kernel_w, np.float32),
        "bias_col": np.ascontiguousarray(bias.reshape(NCH, 128).T, np.float32),
        "ident_in": np.eye(128, dtype=np.float32),
    }



def build_l0attn(TT=2048, NH=8):
    p = Prog()
    SCALE = float(128 ** -0.5)
    x0_d = p.dram("x0", [D, TT], F32, kind="ExternalInput").ap()
    wqt_d = p.dram("WqT", [D, NH * 128], F32, kind="ExternalInput").ap()
    wkt_d = p.dram("WkT", [D, NH * 128], F32, kind="ExternalInput").ap()
    wvt_d = p.dram("WvT", [D, NH * 128], F32, kind="ExternalInput").ap()
    bqt_d = p.dram("BqT", [D, 16], F32, kind="ExternalInput").ap()
    bkt_d = p.dram("BkT", [D, 16], F32, kind="ExternalInput").ap()
    bvt_d = p.dram("BvT", [D, 16], F32, kind="ExternalInput").ap()
    aqt_d = p.dram("AqT", [16, NH * 128], F32, kind="ExternalInput").ap()
    akt_d = p.dram("AkT", [16, NH * 128], F32, kind="ExternalInput").ap()
    avt_d = p.dram("AvT", [16, NH * 128], F32, kind="ExternalInput").ap()
    bq_d = p.dram("bq_col", [128, NH], F32, kind="ExternalInput").ap()
    bk_d = p.dram("bk_col", [128, NH], F32, kind="ExternalInput").ap()
    bv_d = p.dram("bv_row", [1, NH * 128], F32, kind="ExternalInput").ap()
    out_d = p.dram("attnout", [NH * 128, TT], BF16, kind="ExternalOutput").ap()
    q_s = p.dram("q_scr", [NH * 128, TT], BF16).ap()
    k_s = p.dram("k_scr", [NH * 128, TT], BF16).ap()
    v_s = p.dram("v_scr", [TT, NH * 128], BF16).ap()

    cm = Common(p, TT, rows_n=3)
    nr = TT // 512
    NTT = TT // 128

    xn = [p.sb(f"xn_{i}", [128, TT], BF16) for i in range(NCH)]
    rot = [p.sb(f"rot_{i}", [128, TT], F32) for i in range(2)]
    sqf = p.sb("sqf", [128, TT], F32)
    snb = p.sb("snb", [128, 2 * TT], F32)
    sbc = snb[:, 0:TT]
    bbc = snb[:, TT:2 * TT]
    wload = [p.sb(f"wl_{i}", [128, 1024], F32) for i in range(2)]
    wb16 = [p.sb(f"wb_{i}", [128, 1024], BF16) for i in range(2)]
    stg = [p.sb(f"stg_{i}", [128, TT], BF16) for i in range(2)]
    bx = [p.sb(f"bx_{i}", [16, TT], BF16) for i in range(3)]
    bxl = [p.sb(f"bxl_{i}", [128, 16], F32) for i in range(2)]
    bxlb = [p.sb(f"bxlb_{i}", [128, 16], BF16) for i in range(2)]
    al = p.sb("al", [16, NH * 128], F32)
    alb = p.sb("alb", [16, NH * 128], BF16)
    bqc = p.sb("bqc", [128, NH], F32)
    bkc = p.sb("bkc", [128, NH], F32)
    bvr = p.sb("bvr", [1, NH * 128], F32)
    bv_bc = p.sb("bv_bc", [128, NH * 128], F32)
    onesb = p.sb("onesb_a", [128, 1], BF16)
    rot0b = rot[0][:, :].bitcast(BF16)
    rot1b = rot[1][:, :].bitcast(BF16)
    sqfb = sqf[:, :].bitcast(BF16)
    qh = [rot0b[:, 0:TT], rot1b[:, 0:TT]]
    kh = [rot0b[:, TT:2 * TT], rot1b[:, TT:2 * TT]]
    vh = [sqfb[:, 0:NTT * 128], sqfb[:, NTT * 128:2 * NTT * 128]]
    pt = [p.sb(f"pt_{i}", [128, 512], BF16) for i in range(2)]
    rsum = p.sb("rsum", [128, 512], F32)
    srow = p.sb("srow", [1, TT], F32)
    aout = [p.sb(f"ao_{i}", [128, TT], BF16) for i in range(2)]

    PS = [p.ps(f"PS{i}", [128, 512], F32) for i in range(7)]
    cm.init_consts()
    p.on("vector", lambda e: e.memset(onesb[:, :], 1.0), inc=(cm.v, 1))
    p.wait("tensor", cm.v)
    dlb = [p.sem("dlb0"), p.sem("dlb1")]
    dwb = [p.sem("dwb0"), p.sem("dwb1")]
    dal = p.sem("dal")

    for ci in range(NCH):
        xb = rot[ci % 2]
        w = [(cm.t, cm.t.c)] if ci >= 2 else ()
        p.on("sync", lambda e, xb=xb, ci=ci: e.dma_start(
            out=xb[:, :], in_=x0_d[ci * 128:(ci + 1) * 128, :]),
            wait=w, inc=(dlb[ci % 2], 16))
        p.wait("tensor", dlb[ci % 2])
        for r in range(nr):
            p.on("tensor", lambda e, xb=xb, r=r, ci=ci: e.matmul(
                PS[r][0:1, :], cm.ones_col[:, :], xb[:, r * 512:(r + 1) * 512],
                start=(ci == 0), stop=(ci == NCH - 1)), inc=(cm.t, 1))
    s1 = cm.row_at(0)
    b1 = cm.row_at(1)
    m1 = cm.row_at(2)
    nrm = 1.0 / D
    p.wait("vector", cm.t)
    for r in range(nr):
        p.on("vector", lambda e, r=r: e.tensor_single_scalar(
            out=m1[0:1, r * 512:(r + 1) * 512], in_=PS[r][0:1, :], scalar=nrm,
            op=OP.mult), inc=((cm.v, 1) if r == nr - 1 else None))
    for ci in range(NCH):
        xb = rot[ci % 2]
        w = [(cm.t, cm.t.c)] if ci >= 2 else [(cm.v, cm.v.c)]
        p.on("sync", lambda e, xb=xb, ci=ci: e.dma_start(
            out=xb[:, :], in_=x0_d[ci * 128:(ci + 1) * 128, :]),
            wait=w, inc=(dlb[ci % 2], 16))
        p.wait("scalar", dlb[ci % 2])
        p.on("scalar", lambda e, xb=xb: e.activation(out=sqf[:, :], in_=xb[:, :],
                                                     func=AF.Square),
             wait=[(cm.t, cm.t.c)], inc=(cm.s, 1))
        p.wait("tensor", cm.s)
        for r in range(nr):
            p.on("tensor", lambda e, r=r, ci=ci: e.matmul(
                PS[r][0:1, :], cm.ones_col[:, :], sqf[:, r * 512:(r + 1) * 512],
                start=(ci == 0), stop=(ci == NCH - 1)),
                wait=([(cm.v, cm.v.c)] if (ci == 0 and r == 0) else ()),
                inc=(cm.t, 1))
    p.wait("vector", cm.t)
    for r in range(nr):
        p.on("vector", lambda e, r=r: e.tensor_single_scalar(
            out=s1[0:1, r * 512:(r + 1) * 512], in_=PS[r][0:1, :], scalar=nrm,
            op=OP.mult))
    p.on("vector", lambda e: e.tensor_tensor(out=b1[0:1, 0:TT], in0=m1[0:1, 0:TT],
                                             in1=m1[0:1, 0:TT], op=OP.mult))
    p.on("vector", lambda e: e.tensor_tensor(out=s1[0:1, 0:TT], in0=s1[0:1, 0:TT],
                                             in1=b1[0:1, 0:TT], op=OP.subtract))
    p.on("vector", lambda e: e.tensor_single_scalar(out=s1[0:1, 0:TT],
                                                    in_=s1[0:1, 0:TT], scalar=1e-5,
                                                    op=OP.add), inc=(cm.v, 1))
    p.wait("scalar", cm.v)
    p.on("scalar", lambda e: e.activation(out=s1[0:1, 0:TT], in_=s1[0:1, 0:TT],
                                          func=AF.Ln, scale=1.0))
    p.on("scalar", lambda e: e.activation(out=s1[0:1, 0:TT], in_=s1[0:1, 0:TT],
                                          func=AF.Exp, scale=-0.5), inc=(cm.s, 1))
    p.wait("vector", cm.s)
    p.on("vector", lambda e: e.tensor_tensor(out=b1[0:1, 0:TT], in0=m1[0:1, 0:TT],
                                             in1=s1[0:1, 0:TT], op=OP.mult))
    p.on("vector", lambda e: e.tensor_single_scalar(out=b1[0:1, 0:TT],
                                                    in_=b1[0:1, 0:TT], scalar=-1.0,
                                                    op=OP.mult), inc=(cm.v, 1))
    for r in range(nr):
        p.on("tensor", lambda e, r=r: e.matmul(
            PS[4][:, :], cm.ones_row[:, :], s1[0:1, r * 512:(r + 1) * 512],
            start=True, stop=True), wait=[(cm.v, cm.v.c)], inc=(cm.t, 1))
        p.on("tensor", lambda e, r=r: e.matmul(
            PS[5][:, :], cm.ones_row[:, :], b1[0:1, r * 512:(r + 1) * 512],
            start=True, stop=True), inc=(cm.t, 1))
        p.wait("vector", cm.t)
        p.on("vector", lambda e, r=r: e.tensor_copy(
            out=sbc[:, r * 512:(r + 1) * 512], in_=PS[4][:, :]))
        p.on("vector", lambda e, r=r: e.tensor_copy(
            out=bbc[:, r * 512:(r + 1) * 512], in_=PS[5][:, :]), inc=(cm.v, 1))
    for ci in range(NCH):
        xb = rot[ci % 2]
        p.on("sync", lambda e, xb=xb, ci=ci: e.dma_start(
            out=xb[:, :], in_=x0_d[ci * 128:(ci + 1) * 128, :]),
            wait=[(cm.v, cm.v.c)], inc=(dlb[ci % 2], 16))
        p.wait("vector", dlb[ci % 2])
        p.on("vector", lambda e, xb=xb: e.tensor_tensor(
            out=sqf[:, :], in0=xb[:, :], in1=sbc, op=OP.mult))
        p.on("vector", lambda e: e.tensor_tensor(
            out=sqf[:, :], in0=sqf[:, :], in1=bbc, op=OP.add))
        p.on("vector", lambda e, ci=ci: e.tensor_copy(out=xn[ci][:, :], in_=sqf[:, :]),
             inc=(cm.v, 1))
    p.barrier()

    p.on("sync", lambda e: e.dma_start(out=bqc[:, :], in_=bq_d[:, :]), inc=(cm.dl, 16))
    p.on("sync", lambda e: e.dma_start(out=bkc[:, :], in_=bk_d[:, :]), inc=(cm.dl, 16))
    p.on("sync", lambda e: e.dma_start(out=bvr[:, :], in_=bv_d[:, :]), inc=(cm.dl, 16))
    ld0 = cm.dl.c
    p.wait("vector", cm.dl, ld0)
    p.on("vector", lambda e: e.tensor_single_scalar(out=bqc[:, :], in_=bqc[:, :],
                                                    scalar=SCALE, op=OP.mult),
         inc=(cm.v, 1))
    for j in range(NH * 128 // 512):
        p.on("tensor", lambda e, j=j: e.matmul(
            PS[6][:, :], cm.ones_row[:, :], bvr[0:1, j * 512:(j + 1) * 512],
            start=True, stop=True), wait=[(cm.v, cm.v.c)], inc=(cm.t, 1))
        p.wait("vector", cm.t)
        p.on("vector", lambda e, j=j: e.tensor_copy(
            out=bv_bc[:, j * 512:(j + 1) * 512], in_=PS[6][:, :]), inc=(cm.v, 1))

    for i, bt_d in enumerate([bqt_d, bkt_d, bvt_d]):
        for ci in range(NCH):
            lb = bxl[ci % 2]
            lbb = bxlb[ci % 2]
            p.on("gpsimd", lambda e, lb=lb, ci=ci, bt_d=bt_d: e.dma_start(
                out=lb[:, :], in_=bt_d[ci * 128:(ci + 1) * 128, :]),
                wait=[(cm.v, cm.v.c)], inc=(dwb[ci % 2], 16))
            p.wait("vector", dwb[ci % 2])
            p.on("vector", lambda e, lb=lb, lbb=lbb: e.tensor_copy(
                out=lbb[:, :], in_=lb[:, :]),
                wait=[(cm.t, cm.t.c)], inc=(cm.v, 1))
            p.wait("tensor", cm.v)
            for r in range(nr):
                p.on("tensor", lambda e, lbb=lbb, r=r, ci=ci: e.matmul(
                    PS[r][0:16, :],
                    lbb[:, :], xn[ci][:, r * 512:(r + 1) * 512],
                    start=(ci == 0), stop=(ci == NCH - 1)), inc=(cm.t, 1))
        p.wait("scalar", cm.t)
        for r in range(nr):
            p.on("scalar", lambda e, i=i, r=r: e.activation(
                out=bx[i][0:16, r * 512:(r + 1) * 512],
                in_=PS[r][0:16, :],
                func=AF.Identity, scale=0.5), inc=(cm.s, 1))
        p.wait("tensor", cm.s)
    p.barrier()

    st_q = [p.sem("stq0"), p.sem("stq1")]
    for i, (wt_d, at_d, dst_s, bcol) in enumerate(
            [(wqt_d, aqt_d, q_s, bqc), (wkt_d, akt_d, k_s, bkc)]):
        p.on("gpsimd", lambda e, at_d=at_d: e.dma_start(out=al[:, :], in_=at_d[:, :]),
             wait=[(cm.t, cm.t.c)], inc=(dal, 16))
        p.wait("vector", dal)
        p.on("vector", lambda e: e.tensor_copy(out=alb[:, :], in_=al[:, :]),
             inc=(cm.v, 1))
        p.wait("tensor", cm.v)
        for h in range(NH):
            sg = stg[h % 2]
            ssem = st_q[h % 2]
            sprev = ssem.c
            for half in range(2):
                wl = wload[half]
                wbb = wb16[half]
                p.on("gpsimd", lambda e, wl=wl, h=h, half=half, wt_d=wt_d: e.dma_start(
                    out=wl[:, :].rearrange("p (c f) -> p c f", f=128),
                    in_=wt_d[:, h * 128:(h + 1) * 128].rearrange(
                        "(c p) f -> p c f", p=128)[:, half * 8:(half + 1) * 8, :]),
                    wait=[(cm.t, cm.t.c)], inc=(dwb[half], 16))
                p.wait("vector", dwb[half])
                p.on("vector", lambda e, wl=wl, wbb=wbb: e.tensor_copy(
                    out=wbb[:, :], in_=wl[:, :]),
                    wait=[(cm.t, cm.t.c)], inc=(cm.v, 1))
                p.wait("tensor", cm.v)
                for cj in range(8):
                    ci = half * 8 + cj
                    for r in range(nr):
                        p.on("tensor", lambda e, wbb=wbb, cj=cj, ci=ci, r=r: e.matmul(
                            PS[r][:, :], wbb[:, cj * 128:(cj + 1) * 128],
                            xn[ci][:, r * 512:(r + 1) * 512],
                            start=(ci == 0), stop=False), inc=(cm.t, 1))
            for r in range(nr):
                p.on("tensor", lambda e, r=r, h=h, i=i: e.matmul(
                    PS[r][:, :], alb[0:16, h * 128:(h + 1) * 128],
                    bx[i][0:16, r * 512:(r + 1) * 512], start=False, stop=True),
                    inc=(cm.t, 1))
            p.wait("scalar", cm.t)
            for r in range(nr):
                p.on("scalar", lambda e, r=r, h=h, sg=sg, bcol=bcol, i=i: e.activation(
                    out=sg[:, r * 512:(r + 1) * 512],
                    in_=PS[r][:, :], func=AF.Identity, bias=bcol[:, h:h + 1],
                    scale=(SCALE if i == 0 else 1.0)),
                    wait=([(ssem, sprev)] if r == 0 else ()), inc=(cm.s, 1))
            p.wait("sync", cm.s)
            p.on("sync", lambda e, h=h, sg=sg, dst_s=dst_s: e.dma_start(
                out=dst_s[h * 128:(h + 1) * 128, :], in_=sg[:, :]), inc=(ssem, 16))
            p.wait("tensor", cm.s)
    p.barrier()

    p.on("gpsimd", lambda e: e.dma_start(out=al[:, :], in_=avt_d[:, :]),
         wait=[(cm.t, cm.t.c)], inc=(dal, 16))
    p.wait("vector", dal)
    p.on("vector", lambda e: e.tensor_copy(out=alb[:, :], in_=al[:, :]), inc=(cm.v, 1))
    wvh = snb[:, :].bitcast(BF16)[:, 0:NCH * 512]
    st_v = p.sem("stv")
    for dh in range(NH * 128 // 512):
        for ci in range(NCH):
            wl = wload[ci % 2]
            p.on("gpsimd", lambda e, wl=wl, ci=ci, dh=dh: e.dma_start(
                out=wl[:, 0:512], in_=wvt_d[ci * 128:(ci + 1) * 128,
                                            dh * 512:(dh + 1) * 512]),
                wait=[(cm.v, cm.v.c)] if ci >= 2 else [(cm.t, cm.t.c)],
                inc=(dwb[ci % 2], 16))
            p.wait("vector", dwb[ci % 2])
            p.on("vector", lambda e, wl=wl, ci=ci: e.tensor_copy(
                out=wvh[:, ci * 512:(ci + 1) * 512], in_=wl[:, 0:512]),
                wait=([(cm.t, cm.t.c)] if dh > 0 and ci == 0 else ()),
                inc=(cm.v, 1))
        p.wait("tensor", cm.v)
        for tt in range(NTT):
            ps_v = PS[tt % 2]
            stv_prev = st_v.c
            for ci in range(NCH):
                p.on("tensor", lambda e, tt=tt, ci=ci, ps_v=ps_v: e.matmul(
                    ps_v[:, :], xn[ci][:, tt * 128:(tt + 1) * 128],
                    wvh[:, ci * 512:(ci + 1) * 512],
                    start=(ci == 0), stop=False),
                    wait=([(cm.v, cm.v.c)] if ci == 0 else ()), inc=(cm.t, 1))
            p.on("tensor", lambda e, tt=tt, dh=dh, ps_v=ps_v: e.matmul(
                ps_v[:, :], bx[2][0:16, tt * 128:(tt + 1) * 128],
                alb[0:16, dh * 512:(dh + 1) * 512], start=False, stop=True),
                inc=(cm.t, 1))
            p.wait("vector", cm.t)
            p.on("vector", lambda e, tt=tt, dh=dh, ps_v=ps_v: e.tensor_tensor(
                out=stg[tt % 2][:, 0:512], in0=ps_v[:, :],
                in1=bv_bc[:, dh * 512:(dh + 1) * 512], op=OP.add),
                wait=[(st_v, stv_prev), (st_q[tt % 2], st_q[tt % 2].c)],
                inc=(cm.v, 1))
            p.wait("sync", cm.v)
            p.on("sync", lambda e, tt=tt, dh=dh: e.dma_start(
                out=v_s[tt * 128:(tt + 1) * 128, dh * 512:(dh + 1) * 512],
                in_=stg[tt % 2][:, 0:512]), inc=(st_v, 16))
    p.barrier()

    ldh = [p.sem("ldh0"), p.sem("ldh1")]

    def fetch_head(h):
        b = h % 2
        p.on("gpsimd", lambda e, h=h, b=b: e.dma_start(
            out=qh[b], in_=q_s[h * 128:(h + 1) * 128, :]),
            wait=[(cm.t, cm.t.c)], inc=(ldh[b], 16))
        p.on("gpsimd", lambda e, h=h, b=b: e.dma_start(
            out=kh[b], in_=k_s[h * 128:(h + 1) * 128, :]), inc=(ldh[b], 16))
        p.on("gpsimd", lambda e, h=h, b=b: e.dma_start(
            out=vh[b].rearrange("p (tt f) -> p tt f", f=128),
            in_=v_s.rearrange("(tt p) dv -> p tt dv", p=128)[
                :, :, h * 128:(h + 1) * 128]), inc=(ldh[b], 16))

    p.wait("gpsimd", st_v)
    p.wait("gpsimd", st_q[0])
    p.wait("gpsimd", st_q[1])
    fetch_head(0)
    st_a = [p.sem("sta0"), p.sem("sta1")]
    for h in range(NH):
        b = h % 2
        ld_tgt = ldh[b].c
        if h + 1 < NH:
            fetch_head(h + 1)
        p.wait("tensor", ldh[b], ld_tgt)
        for r in range(nr):
            for tk in range(NTT):
                ptile = pt[tk % 2]
                p.on("tensor", lambda e, b=b, r=r, tk=tk: e.matmul(
                    PS[3][:, :], kh[b][:, tk * 128:(tk + 1) * 128],
                    qh[b][:, r * 512:(r + 1) * 512],
                    start=True, stop=True),
                    wait=[(cm.s, cm.s.c)] + ([(cm.v, cm.v.c)] if tk == 0 else []),
                    inc=(cm.t, 1))
                p.wait("scalar", cm.t)
                p.on("scalar", lambda e, ptile=ptile: e.activation(
                    out=ptile[:, :], in_=PS[3][:, :], func=AF.Exp),
                    wait=[(cm.t, cm.t.c)], inc=(cm.s, 1))
                p.wait("tensor", cm.s)
                p.on("tensor", lambda e, b=b, tk=tk, ptile=ptile: e.matmul(
                    PS[4][:, :], vh[b][:, tk * 128:(tk + 1) * 128], ptile[:, :],
                    start=(tk == 0), stop=(tk == NTT - 1)), inc=(cm.t, 1))
                p.on("tensor", lambda e, tk=tk, ptile=ptile: e.matmul(
                    PS[5][0:1, :], onesb[:, :], ptile[:, :],
                    start=(tk == 0), stop=(tk == NTT - 1)), inc=(cm.t, 1))
            p.wait("vector", cm.t)
            p.on("vector", lambda e, r=r: e.reciprocal(
                out=srow[0:1, r * 512:(r + 1) * 512], in_=PS[5][0:1, :]),
                inc=(cm.v, 1))
            p.on("tensor", lambda e, r=r: e.matmul(
                PS[6][:, :], cm.ones_row[:, :], srow[0:1, r * 512:(r + 1) * 512],
                start=True, stop=True), wait=[(cm.v, cm.v.c)], inc=(cm.t, 1))
            p.wait("vector", cm.t)
            p.on("vector", lambda e: e.tensor_copy(out=rsum[:, :], in_=PS[6][:, :]))
            p.on("vector", lambda e, b=b, r=r: e.tensor_tensor(
                out=aout[b][:, r * 512:(r + 1) * 512], in0=PS[4][:, :],
                in1=rsum[:, :], op=OP.mult),
                wait=[(st_a[b], st_a[b].c)] if r == 0 else (), inc=(cm.v, 1))
        p.wait("sync", cm.v)
        p.on("sync", lambda e, h=h, b=b: e.dma_start(
            out=out_d[h * 128:(h + 1) * 128, :], in_=aout[b][:, :]),
            inc=(st_a[b], 16))
    p.wait("sync", st_a[0])
    p.wait("sync", st_a[1])
    p.wait("gpsimd", st_a[0])
    p.wait("gpsimd", st_a[1])
    return p.build()


def host_l0attn_inputs(x0_fm, Wq, bq, Bq, Aq, Wk, bk, Bk, Ak, Wv, bv, Bv, Av, role):
    h0 = role * 1024
    sl = slice(h0, h0 + 1024)
    f32 = np.float32

    def C(a):
        return np.ascontiguousarray(a, f32)
    return {
        "x0": C(x0_fm),
        "WqT": C(Wq[sl, :].T), "WkT": C(Wk[sl, :].T), "WvT": C(Wv[sl, :].T),
        "BqT": C(Bq.T), "BkT": C(Bk.T), "BvT": C(Bv.T),
        "AqT": C(Aq[sl, :].T), "AkT": C(Ak[sl, :].T), "AvT": C(Av[sl, :].T),
        "bq_col": C(bq[sl].reshape(8, 128).T),
        "bk_col": C(bk[sl].reshape(8, 128).T),
        "bv_row": C(bv[sl].reshape(1, 1024)),
    }



def build_projlnres(T=T_OWN, lora=True):
    p = Prog()
    ain_d = p.dram("ain", [D, T], BF16, kind="ExternalInput").ap()
    xr_d = p.dram("xres", [D, T], F32, kind="ExternalInput").ap()
    wt_d = p.dram("WT", [D, D], F32, kind="ExternalInput").ap()
    if lora:
        bot_d = p.dram("BoT", [D, 16], F32, kind="ExternalInput").ap()
        aot_d = p.dram("AoT", [16, D], F32, kind="ExternalInput").ap()
        bo_d = p.dram("bo_col", [128, NCH], F32, kind="ExternalInput").ap()
    xo_d = p.dram("xout", [D, T], F32, kind="ExternalOutput").ap()

    cm = Common(p, T, rows_n=3)
    nr = T // 512

    ain = [p.sb(f"ain_{i}", [128, T], BF16) for i in range(NCH)]
    xr = [p.sb(f"xr_{i}", [128, T], F32) for i in range(NCH)]
    outb = [p.sb(f"outb_{i}", [128, T], BF16) for i in range(NCH)]
    sqb = [p.sb(f"sq_{i}", [128, T], F32) for i in range(2)]
    sqbb = [sqb[i][:, :].bitcast(BF16)[:, 0:T] for i in range(2)]
    rot = [p.sb(f"rot_{i}", [128, T], F32) for i in range(2)]
    wf = [p.sb(f"wf_{i}", [128, D], F32) for i in range(2)]
    wb16 = [p.sb(f"wb16_{i}", [128, D], BF16) for i in range(2)]
    onesb = p.sb("onesb", [128, 1], BF16)
    if lora:
        bx = p.sb("bx", [16, T], BF16)
        bxl = [p.sb(f"bxl_{i}", [128, 16], F32) for i in range(2)]
        bxlb = [p.sb(f"bxlb_{i}", [128, 16], BF16) for i in range(2)]
        al = p.sb("al", [16, D], F32)
        alb = p.sb("alb", [16, D], BF16)
        boc = p.sb("boc", [128, NCH], F32)

    P = [p.ps(f"P{i}", [128, 512], F32) for i in range(6)]
    cm.init_consts()
    p.on("vector", lambda e: e.memset(onesb[:, :], 1.0), inc=(cm.v, 1))
    p.wait("tensor", cm.v)
    dwb = [p.sem("dwb0"), p.sem("dwb1")]
    dwg = [p.sem("dwg0"), p.sem("dwg1")]
    dal = p.sem("dal")

    fm_load(p, cm, ain, ain_d, T)
    fm_load(p, cm, xr, xr_d, T)
    if lora:
        p.on("sync", lambda e: e.dma_start(out=boc[:, :], in_=bo_d[:, :]),
             inc=(cm.dl, 16))
    in_ready = cm.dl.c
    p.wait("tensor", cm.dl, in_ready)
    p.wait("vector", cm.dl, in_ready)
    p.wait("scalar", cm.dl, in_ready)

    if lora:
        p.on("gpsimd", lambda e: e.dma_start(out=al[:, :], in_=aot_d[:, :]),
             inc=(dal, 16))
        p.wait("vector", dal)
        p.on("vector", lambda e: e.tensor_copy(out=alb[:, :], in_=al[:, :]),
             inc=(cm.v, 1))
        for ci in range(NCH):
            lb = bxl[ci % 2]
            lbb = bxlb[ci % 2]
            p.on("gpsimd", lambda e, lb=lb, ci=ci: e.dma_start(
                out=lb[:, :], in_=bot_d[ci * 128:(ci + 1) * 128, :]),
                wait=[(cm.v, cm.v.c)], inc=(dwb[ci % 2], 16))
            p.wait("vector", dwb[ci % 2])
            p.on("vector", lambda e, lb=lb, lbb=lbb: e.tensor_copy(
                out=lbb[:, :], in_=lb[:, :]), wait=[(cm.t, cm.t.c)], inc=(cm.v, 1))
            p.wait("tensor", cm.v)
            for r in range(nr):
                p.on("tensor", lambda e, lbb=lbb, r=r, ci=ci: e.matmul(
                    P[r][0:16, :], lbb[:, :], ain[ci][:, r * 512:(r + 1) * 512],
                    start=(ci == 0), stop=(ci == NCH - 1)), inc=(cm.t, 1))
        p.wait("scalar", cm.t)
        for r in range(nr):
            p.on("scalar", lambda e, r=r: e.activation(
                out=bx[0:16, r * 512:(r + 1) * 512], in_=P[r][0:16, :],
                func=AF.Identity, scale=0.5), inc=(cm.s, 1))
        p.wait("tensor", cm.s)

    wld = {}
    for g in range(NCH):
        wb = wf[g % 2]
        wbb = wb16[g % 2]
        wget = [(cm.v, wld[g - 2])] if g >= 2 else ()
        for ci in range(NCH):
            p.on("gpsimd", lambda e, wb=wb, g=g, ci=ci: e.dma_start(
                out=wb[:, ci * 128:(ci + 1) * 128],
                in_=wt_d[ci * 128:(ci + 1) * 128, g * 128:(g + 1) * 128]),
                wait=(wget if ci == 0 else ()), inc=(dwg[g % 2], 16))
        p.wait("vector", dwg[g % 2])
        p.on("vector", lambda e, wb=wb, wbb=wbb: e.tensor_copy(out=wbb[:, :],
                                                              in_=wb[:, :]),
             wait=[(cm.t, cm.t.c)], inc=(cm.v, 1))
        wld[g] = cm.v.c
        p.wait("tensor", cm.v)
        for r in range(nr):
            ps_o = P[4 + r]
            for ci in range(NCH):
                last = (ci == NCH - 1) and not lora
                p.on("tensor", lambda e, ps_o=ps_o, wbb=wbb, ci=ci, r=r, last=last: e.matmul(
                    ps_o[:, :], wbb[:, ci * 128:(ci + 1) * 128],
                    ain[ci][:, r * 512:(r + 1) * 512],
                    start=(ci == 0), stop=last),
                    wait=([(cm.s, cm.s.c)] if ci == 0 else ()), inc=(cm.t, 1))
            if lora:
                p.on("tensor", lambda e, ps_o=ps_o, g=g, r=r: e.matmul(
                    ps_o[:, :], alb[0:16, g * 128:(g + 1) * 128],
                    bx[0:16, r * 512:(r + 1) * 512], start=False, stop=True),
                    inc=(cm.t, 1))
            p.wait("scalar", cm.t)
            if lora:
                p.on("scalar", lambda e, g=g, r=r, ps_o=ps_o: e.activation(
                    out=outb[g][:, r * 512:(r + 1) * 512], in_=ps_o[:, :],
                    func=AF.Identity, bias=boc[:, g:g + 1], scale=1.0),
                    inc=(cm.s, 1))
            else:
                p.on("scalar", lambda e, g=g, r=r, ps_o=ps_o: e.activation(
                    out=outb[g][:, r * 512:(r + 1) * 512], in_=ps_o[:, :],
                    func=AF.Identity, scale=1.0), inc=(cm.s, 1))
    p.barrier()

    for r in range(nr):
        for ci in range(NCH):
            p.on("tensor", lambda e, r=r, ci=ci: e.matmul(
                P[r][0:1, :], onesb[:, :], outb[ci][:, r * 512:(r + 1) * 512],
                start=(ci == 0), stop=(ci == NCH - 1)),
                inc=((cm.t, 1) if ci == NCH - 1 else None))
    mm_done = {}
    for ci in range(NCH):
        w = [(cm.t, mm_done[ci - 2])] if ci >= 2 else ()
        p.on("scalar", lambda e, ci=ci: e.activation(
            out=sqbb[ci % 2], in_=outb[ci][:, :], func=AF.Square),
            wait=w, inc=(cm.s, 1))
        p.wait("tensor", cm.s)
        for r in range(nr):
            p.on("tensor", lambda e, r=r, ci=ci: e.matmul(
                P[nr + r][0:1, :], onesb[:, :], sqbb[ci % 2][:, r * 512:(r + 1) * 512],
                start=(ci == 0), stop=(ci == NCH - 1)), inc=(cm.t, 1))
        mm_done[ci] = cm.t.c
    s3 = cm.row()
    b3 = cm.row()
    ln_rows(p, cm, P, T, s3, b3, eps=1e-5, dim=D, wait=[(cm.t, cm.t.c)])
    bcast_rows(p, cm, s3, T, [P[0], P[1]], wait=[(cm.v, cm.v.c)])
    bcast_rows(p, cm, b3, T, [P[2], P[3]])
    p.wait("vector", cm.t)

    st_sems = [p.sem("st0"), p.sem("st1")]
    st_done = {}
    for ci in range(NCH):
        rb = rot[ci % 2]
        w = [(st_sems[ci % 2], st_done[ci - 2])] if ci >= 2 else ()
        for r in range(nr):
            p.on("vector", lambda e, rb=rb, ci=ci, r=r: e.tensor_tensor(
                out=rb[:, r * 512:(r + 1) * 512], in0=outb[ci][:, r * 512:(r + 1) * 512],
                in1=P[r][:, :], op=OP.mult), wait=(w if r == 0 else ()))
            p.on("vector", lambda e, rb=rb, ci=ci, r=r: e.tensor_tensor(
                out=rb[:, r * 512:(r + 1) * 512], in0=rb[:, r * 512:(r + 1) * 512],
                in1=P[2 + r][:, :], op=OP.add))
        p.on("vector", lambda e, rb=rb, ci=ci: e.tensor_tensor(
            out=rb[:, :], in0=rb[:, :], in1=xr[ci][:, :], op=OP.add), inc=(cm.v, 1))
        p.wait("sync", cm.v)
        p.on("sync", lambda e, rb=rb, ci=ci: e.dma_start(
            out=xo_d[ci * 128:(ci + 1) * 128, 0:T], in_=rb[:, :]),
            inc=(st_sems[ci % 2], 16))
        st_done[ci] = st_sems[ci % 2].c
    p.wait("sync", st_sems[0])
    p.wait("sync", st_sems[1])
    p.wait("gpsimd", st_sems[0])
    p.wait("gpsimd", st_sems[1])
    return p.build()


def host_projlnres_inputs(ain_bf16_fm, xres_fm, W, bo=None, Bo=None, Ao=None):
    d = {
        "ain": ain_bf16_fm,
        "xres": np.ascontiguousarray(xres_fm, np.float32),
        "WT": np.ascontiguousarray(W.T, np.float32),
    }
    if bo is not None:
        d["BoT"] = np.ascontiguousarray(Bo.T, np.float32)
        d["AoT"] = np.ascontiguousarray(Ao.T, np.float32)
        d["bo_col"] = np.ascontiguousarray(bo.reshape(NCH, 128).T, np.float32)
    return d



def build_l1attn(TT=2048, NH=8, dbg=False, phases=3, qkn=None, qkstage=9, nci_lim=None, ci_stage=99):
    p = Prog()
    NTT = TT // 128
    nr = TT // 512
    C = 32
    NCI = TT // C
    x2_d = p.dram("x2", [D, TT], F32, kind="ExternalInput").ap()
    wqt_d = p.dram("WqT", [D, NH * 128], F32, kind="ExternalInput").ap()
    wkt_d = p.dram("WkT", [D, NH * 512], F32, kind="ExternalInput").ap()
    wvt_d = p.dram("WvT", [D, NH * 512], F32, kind="ExternalInput").ap()
    cqd_d = p.dram("cq_diag", [NH * 4 * 128, 128], F32, kind="ExternalInput").ap()
    ckd_d = p.dram("ck_diag", [NH * 16 * 128, 128], F32, kind="ExternalInput").ap()
    cvd_d = p.dram("cv_diag", [NH * 16 * 128, 128], F32, kind="ExternalInput").ap()
    wf1t_d = p.dram("Wf1T", [D, 128], F32, kind="ExternalInput").ap()
    wf2t_d = p.dram("Wf2T", [128, NH * 128], F32, kind="ExternalInput").ap()
    wbt_d = p.dram("WbT", [D, NH * 4], F32, kind="ExternalInput").ap()
    alog_d = p.dram("nAexp", [1, NH], F32, kind="ExternalInput").ap()
    dtb_d = p.dram("dtb_col", [128, NH], F32, kind="ExternalInput").ap()
    wg1t_d = p.dram("Wg1T", [D, 128], F32, kind="ExternalInput").ap()
    bg1_d = p.dram("bg1_col", [128, 1], F32, kind="ExternalInput").ap()
    wg2t_d = p.dram("Wg2T", [128, NH * 128], F32, kind="ExternalInput").ap()
    bg2_d = p.dram("bg2_col", [128, NH], F32, kind="ExternalInput").ap()
    onw_d = p.dram("onorm_col", [128, 1], F32, kind="ExternalInput").ap()
    id_d = p.dram("ident_in", [128, 128], F32, kind="ExternalInput").ap()
    out_d = p.dram("ogate", [NH * 128, TT], BF16, kind="ExternalOutput").ap()
    k_s = p.dram("k_scr", [NH * 4 * 128, TT], BF16).ap()
    kb_s = p.dram("kb_scr", [NH * 4 * 128, TT], BF16).ap()
    v_s = p.dram("v_scr", [TT * 4, NH * 128], BF16).ap()
    q_s = p.dram("q_scr", [NH * 128, TT], BF16).ap()
    g_s = p.dram("g_scr", [NH * 128, TT], F32).ap()
    gate_s = p.dram("gate_scr", [NH * 128, TT], BF16).ap()
    beta_s = p.dram("beta_scr", [NH * 4, TT], F32).ap()

    cm = Common(p, TT, rows_n=0)
    dlb = [p.sem("dlb0"), p.sem("dlb1")]
    dwb = [p.sem("dwb0"), p.sem("dwb1")]
    dal = p.sem("dal")
    dcon = p.sem("dcon")
    stp = [p.sem("stp0"), p.sem("stp1")]
    stpA = [p.sem("stpA0"), p.sem("stpA1")]
    stpB = [p.sem("stpB0"), p.sem("stpB1")]
    stv2 = p.sem("stv2")
    lastA = {0: 0, 1: 0}
    lastB = {0: 0, 1: 0}
    lastV = [0]

    AR = p.sb("AR", [128, 88 * 1024], BF16)

    def cv(byte_off, cols, dt=BF16):
        n = cols if dt == BF16 else cols * 2
        a = AR[:, byte_off // 2:byte_off // 2 + n]
        return a if dt == BF16 else a.bitcast(dt)

    xn = [cv(i * 4096, TT) for i in range(NCH)]
    rot = [cv(65536, TT, F32), cv(65536 + 8192, TT, F32)]
    sqf = cv(81920, TT, F32)
    snb = cv(90112, 2 * TT, F32)
    sbc = snb[:, 0:TT]
    bbc = snb[:, TT:2 * TT]
    wload = [p.sb(f"wl_{i}", [128, 1024], F32) for i in range(2)]
    wb16 = [p.sb(f"wb_{i}", [128, 1024], BF16) for i in range(2)]
    stg = [cv(106496, TT + 8), cv(111104, TT + 8)]
    stg2 = [cv(115712, TT), cv(115712 + 4096, TT)]
    cdg = [cv(123904, 512), cv(123904 + 1024, 512)]
    rowbuf = p.sb("rowbuf", [1, TT], F32)
    colb = p.sb("colb", [128, 16], F32)
    PS = [p.ps(f"PS{i}", [128, 512], F32) for i in range(7)]
    cm.init_consts(identity_input_ap=id_d)
    p.on("vector", lambda e: e.memset(AR[:, 0:45056], 0.0))
    p.on("vector", lambda e: e.memset(AR[:, 45056:90112], 0.0), inc=(cm.v, 1))
    p.wait("tensor", cm.v)
    p.wait("tensor", cm.di)

    for ci in range(NCH):
        xb = rot[ci % 2]
        w = [(cm.t, cm.t.c)] if ci >= 2 else [(cm.v, cm.v.c)]
        p.on("sync", lambda e, xb=xb, ci=ci: e.dma_start(
            out=xb[:, :], in_=x2_d[ci * 128:(ci + 1) * 128, :]),
            wait=w, inc=(dlb[ci % 2], 16))
        p.wait("tensor", dlb[ci % 2])
        for r in range(nr):
            p.on("tensor", lambda e, xb=xb, r=r, ci=ci: e.matmul(
                PS[r][0:1, :], cm.ones_col[:, :], xb[:, r * 512:(r + 1) * 512],
                start=(ci == 0), stop=(ci == NCH - 1)), inc=(cm.t, 1))
    s1 = cv(154624, TT, F32)[0:1, :]
    b1 = cv(162816, TT, F32)[0:1, :]
    m1 = cv(171008, TT, F32)[0:1, :]
    nrm = 1.0 / D
    p.wait("vector", cm.t)
    for r in range(nr):
        p.on("vector", lambda e, r=r: e.tensor_single_scalar(
            out=m1[0:1, r * 512:(r + 1) * 512], in_=PS[r][0:1, :], scalar=nrm,
            op=OP.mult), inc=((cm.v, 1) if r == nr - 1 else None))
    for ci in range(NCH):
        xb = rot[ci % 2]
        w = [(cm.t, cm.t.c)] if ci >= 2 else [(cm.v, cm.v.c)]
        p.on("sync", lambda e, xb=xb, ci=ci: e.dma_start(
            out=xb[:, :], in_=x2_d[ci * 128:(ci + 1) * 128, :]),
            wait=w, inc=(dlb[ci % 2], 16))
        p.wait("scalar", dlb[ci % 2])
        p.on("scalar", lambda e, xb=xb: e.activation(out=sqf[:, :], in_=xb[:, :],
                                                     func=AF.Square),
             wait=[(cm.t, cm.t.c)], inc=(cm.s, 1))
        p.wait("tensor", cm.s)
        for r in range(nr):
            p.on("tensor", lambda e, r=r, ci=ci: e.matmul(
                PS[r][0:1, :], cm.ones_col[:, :], sqf[:, r * 512:(r + 1) * 512],
                start=(ci == 0), stop=(ci == NCH - 1)),
                wait=([(cm.v, cm.v.c)] if (ci == 0 and r == 0) else ()),
                inc=(cm.t, 1))
    p.wait("vector", cm.t)
    for r in range(nr):
        p.on("vector", lambda e, r=r: e.tensor_single_scalar(
            out=s1[0:1, r * 512:(r + 1) * 512], in_=PS[r][0:1, :], scalar=nrm,
            op=OP.mult))
    p.on("vector", lambda e: e.tensor_tensor(out=b1[0:1, 0:TT], in0=m1[0:1, 0:TT],
                                             in1=m1[0:1, 0:TT], op=OP.mult))
    p.on("vector", lambda e: e.tensor_tensor(out=s1[0:1, 0:TT], in0=s1[0:1, 0:TT],
                                             in1=b1[0:1, 0:TT], op=OP.subtract))
    p.on("vector", lambda e: e.tensor_single_scalar(out=s1[0:1, 0:TT],
                                                    in_=s1[0:1, 0:TT], scalar=1e-5,
                                                    op=OP.add), inc=(cm.v, 1))
    p.wait("scalar", cm.v)
    p.on("scalar", lambda e: e.activation(out=s1[0:1, 0:TT], in_=s1[0:1, 0:TT],
                                          func=AF.Ln, scale=1.0))
    p.on("scalar", lambda e: e.activation(out=s1[0:1, 0:TT], in_=s1[0:1, 0:TT],
                                          func=AF.Exp, scale=-0.5), inc=(cm.s, 1))
    p.wait("vector", cm.s)
    p.on("vector", lambda e: e.tensor_tensor(out=b1[0:1, 0:TT], in0=m1[0:1, 0:TT],
                                             in1=s1[0:1, 0:TT], op=OP.mult))
    p.on("vector", lambda e: e.tensor_single_scalar(out=b1[0:1, 0:TT],
                                                    in_=b1[0:1, 0:TT], scalar=-1.0,
                                                    op=OP.mult), inc=(cm.v, 1))
    for r in range(nr):
        p.on("tensor", lambda e, r=r: e.matmul(
            PS[4][:, :], cm.ones_row[:, :], s1[0:1, r * 512:(r + 1) * 512],
            start=True, stop=True), wait=[(cm.v, cm.v.c)], inc=(cm.t, 1))
        p.on("tensor", lambda e, r=r: e.matmul(
            PS[5][:, :], cm.ones_row[:, :], b1[0:1, r * 512:(r + 1) * 512],
            start=True, stop=True), inc=(cm.t, 1))
        p.wait("vector", cm.t)
        p.on("vector", lambda e, r=r: e.tensor_copy(
            out=sbc[:, r * 512:(r + 1) * 512], in_=PS[4][:, :]))
        p.on("vector", lambda e, r=r: e.tensor_copy(
            out=bbc[:, r * 512:(r + 1) * 512], in_=PS[5][:, :]), inc=(cm.v, 1))
    for ci in range(NCH):
        xb = rot[ci % 2]
        p.on("sync", lambda e, xb=xb, ci=ci: e.dma_start(
            out=xb[:, :], in_=x2_d[ci * 128:(ci + 1) * 128, :]),
            wait=[(cm.v, cm.v.c)], inc=(dlb[ci % 2], 16))
        p.wait("vector", dlb[ci % 2])
        p.on("vector", lambda e, xb=xb: e.tensor_tensor(
            out=sqf[:, :], in0=xb[:, :], in1=sbc, op=OP.mult))
        p.on("vector", lambda e: e.tensor_tensor(
            out=sqf[:, :], in0=sqf[:, :], in1=bbc, op=OP.add))
        p.on("vector", lambda e, ci=ci: e.tensor_copy(out=xn[ci][:, :], in_=sqf[:, :]),
             inc=(cm.v, 1))
    p.barrier()

    cmask = cv(142336, TT, F32)
    sqb_l2 = p.sb("sqb_l2", [128, 512], BF16)
    onesb_d = p.sb("onesb_d", [128, 1], BF16)
    betasb = cv(134144, TT, F32)
    f1t = cv(125952, TT)
    g1t = cv(130048, TT)
    negA = p.sb("negA", [128, NH], F32)
    dtbc = p.sb("dtbc", [128, NH], F32)
    bg1c = p.sb("bg1c", [128, 1], F32)
    bg2c = p.sb("bg2c", [128, NH], F32)
    arow = p.sb("arow", [1, NH], F32)
    vtb = p.sb("vtb", [128, 128], BF16)

    p.on("vector", lambda e: e.memset(onesb_d[:, :], 1.0))
    p.on("vector", lambda e: e.memset(cmask[:, :], 1.0))
    p.on("vector", lambda e: e.memset(
        cmask[:, :].rearrange("p (n c) -> p n c", c=C)[:, :, 0:1], 0.0),
        inc=(cm.v, 1))
    p.on("sync", lambda e: e.dma_start(out=arow[:, :], in_=alog_d[:, :]), inc=(cm.dl, 16))
    p.on("sync", lambda e: e.dma_start(out=dtbc[:, :], in_=dtb_d[:, :]), inc=(cm.dl, 16))
    p.on("sync", lambda e: e.dma_start(out=bg1c[:, :], in_=bg1_d[:, :]), inc=(cm.dl, 16))
    p.on("sync", lambda e: e.dma_start(out=bg2c[:, :], in_=bg2_d[:, :]), inc=(cm.dl, 16))
    small_ld = cm.dl.c
    p.wait("scalar", cm.dl, small_ld)
    p.on("scalar", lambda e: e.activation(out=arow[0:1, :], in_=arow[0:1, :],
                                          func=AF.Exp), inc=(cm.s, 1))
    p.wait("vector", cm.s)
    p.on("vector", lambda e: e.tensor_single_scalar(out=arow[0:1, :], in_=arow[0:1, :],
                                                    scalar=-1.0, op=OP.mult),
         inc=(cm.v, 1))
    p.on("tensor", lambda e: e.matmul(PS[6][:, 0:NH], cm.ones_row[:, :], arow[0:1, :],
                                      start=True, stop=True),
         wait=[(cm.v, cm.v.c)], inc=(cm.t, 1))
    p.wait("vector", cm.t)
    p.on("vector", lambda e: e.tensor_copy(out=negA[:, :], in_=PS[6][:, 0:NH]),
         inc=(cm.v, 1))

    if phases == 10:
        p.barrier()
        return p.build()
    for ci in range(NCH):
        wl = wload[ci % 2]
        p.on("gpsimd", lambda e, wl=wl, ci=ci: e.dma_start(
            out=wl[:, 0:32], in_=wbt_d[ci * 128:(ci + 1) * 128, :]),
            wait=[(cm.v, cm.v.c)], inc=(dwb[ci % 2], 16))
        p.wait("vector", dwb[ci % 2])
        p.on("vector", lambda e, wl=wl, ci=ci: e.tensor_copy(
            out=wb16[ci % 2][:, 0:32], in_=wl[:, 0:32]),
            wait=[(cm.t, cm.t.c)], inc=(cm.v, 1))
        p.wait("tensor", cm.v)
        for r in range(nr):
            p.on("tensor", lambda e, ci=ci, r=r: e.matmul(
                PS[r][0:32, :], wb16[ci % 2][:, 0:32],
                xn[ci][:, r * 512:(r + 1) * 512],
                start=(ci == 0), stop=(ci == NCH - 1)), inc=(cm.t, 1))
    p.wait("scalar", cm.t)
    for r in range(nr):
        p.on("scalar", lambda e, r=r: e.activation(
            out=betasb[0:32, r * 512:(r + 1) * 512], in_=PS[r][0:32, :],
            func=AF.Sigmoid), inc=(cm.s, 1))
    p.wait("sync", cm.s)
    p.on("sync", lambda e: e.dma_start(out=beta_s[:, :], in_=betasb[0:32, :]),
         inc=(stp[0], 16))

    if phases == 11:
        p.barrier()
        return p.build()
    p.wait("tensor", cm.s)
    for dst, wsrc, bias in ((f1t, wf1t_d, None), (g1t, wg1t_d, bg1c)):
        for ci in range(NCH):
            wl = wload[ci % 2]
            p.on("gpsimd", lambda e, wl=wl, ci=ci, wsrc=wsrc: e.dma_start(
                out=wl[:, 0:128], in_=wsrc[ci * 128:(ci + 1) * 128, :]),
                wait=[(cm.v, cm.v.c)], inc=(dwb[ci % 2], 16))
            p.wait("vector", dwb[ci % 2])
            p.on("vector", lambda e, wl=wl, ci=ci: e.tensor_copy(
                out=wb16[ci % 2][:, 0:128], in_=wl[:, 0:128]),
                wait=[(cm.t, cm.t.c)], inc=(cm.v, 1))
            p.wait("tensor", cm.v)
            for r in range(nr):
                p.on("tensor", lambda e, ci=ci, r=r: e.matmul(
                    PS[r][:, :], wb16[ci % 2][:, 0:128],
                    xn[ci][:, r * 512:(r + 1) * 512],
                    start=(ci == 0), stop=(ci == NCH - 1)), inc=(cm.t, 1))
        p.wait("scalar", cm.t)
        for r in range(nr):
            if bias is None:
                p.on("scalar", lambda e, r=r, dst=dst: e.activation(
                    out=dst[:, r * 512:(r + 1) * 512], in_=PS[r][:, :],
                    func=AF.Identity), inc=(cm.s, 1))
            else:
                p.on("scalar", lambda e, r=r, dst=dst, bias=bias: e.activation(
                    out=dst[:, r * 512:(r + 1) * 512], in_=PS[r][:, :],
                    func=AF.Identity, bias=bias[:, 0:1]), inc=(cm.s, 1))
        p.wait("tensor", cm.s)

    wf2sb = cv(150528, NH * 128)
    wg2sb = cv(152576, NH * 128)
    for wsb, wsrc in ((wf2sb, wf2t_d), (wg2sb, wg2t_d)):
        p.on("gpsimd", lambda e, wsb=wsb, wsrc=wsrc: e.dma_start(
            out=wload[0][:, :], in_=wsrc[:, 0:1024]), wait=[(cm.v, cm.v.c)],
            inc=(dal, 16))
        p.wait("vector", dal)
        p.on("vector", lambda e, wsb=wsb: e.tensor_copy(out=wsb[:, 0:1024],
                                                        in_=wload[0][:, :]),
             inc=(cm.v, 1))
    if phases == 12:
        p.barrier()
        return p.build()
    la_f = sqf
    st_g = p.sem("st_g")
    st_g2 = p.sem("st_g2")
    for h in range(NH):
        for r in range(nr):
            p.on("tensor", lambda e, h=h, r=r: e.matmul(
                PS[r][:, :], wf2sb[:, h * 128:(h + 1) * 128],
                f1t[:, r * 512:(r + 1) * 512], start=True, stop=True),
                wait=([(cm.v, cm.v.c), (cm.s, cm.s.c)] if r == 0 else ()),
                inc=(cm.t, 1))
        p.wait("scalar", cm.t)
        for r in range(nr):
            p.on("scalar", lambda e, h=h, r=r: e.activation(
                out=la_f[:, r * 512:(r + 1) * 512], in_=PS[r][:, :],
                func=AF.Identity, bias=dtbc[:, h:h + 1]), inc=(cm.s, 1))
        p.wait("vector", cm.s)
        t2 = rot[1]
        p.on("vector", lambda e: e.tensor_single_scalar(
            out=t2[:, :], in_=la_f[:, :], scalar=0.0, op=OP.max))
        p.on("scalar", lambda e: e.activation(out=la_f[:, :], in_=la_f[:, :],
                                              func=AF.Abs), wait=[(cm.v, cm.v.c)],
             inc=(cm.s, 1))
        p.on("scalar", lambda e: e.activation(out=la_f[:, :], in_=la_f[:, :],
                                              func=AF.Exp, scale=-1.0), inc=(cm.s, 1))
        p.wait("vector", cm.s)
        p.on("vector", lambda e: e.tensor_single_scalar(
            out=la_f[:, :], in_=la_f[:, :], scalar=1.0, op=OP.add), inc=(cm.v, 1))
        p.wait("scalar", cm.v)
        p.on("scalar", lambda e: e.activation(out=la_f[:, :], in_=la_f[:, :],
                                              func=AF.Ln), inc=(cm.s, 1))
        p.wait("vector", cm.s)
        p.on("vector", lambda e: e.tensor_tensor(out=la_f[:, :], in0=la_f[:, :],
                                                 in1=t2[:, :], op=OP.add))
        p.on("vector", lambda e, h=h: e.tensor_tensor(
            out=la_f[:, :], in0=la_f[:, :],
            in1=negA[:, h:h + 1].broadcast_to([128, TT]), op=OP.mult))
        p.on("vector", lambda e, t2=t2: e.tensor_tensor_scan(
            out=t2[:, :], data0=cmask[:, :], data1=la_f[:, :],
            initial=bg1c[:, 0:1], op0=OP.mult, op1=OP.add), inc=(cm.v, 1))
        p.wait("sync", cm.v)
        p.on("sync", lambda e, h=h, t2=t2: e.dma_start(
            out=g_s[h * 128:(h + 1) * 128, :], in_=t2[:, :]), inc=(st_g, 16))
        p.wait("vector", st_g)
        for r in range(nr):
            p.on("tensor", lambda e, h=h, r=r: e.matmul(
                PS[r][:, :], wg2sb[:, h * 128:(h + 1) * 128],
                g1t[:, r * 512:(r + 1) * 512], start=True, stop=True),
                wait=([(cm.s, cm.s.c)] if r == 0 else ()), inc=(cm.t, 1))
        p.wait("scalar", cm.t)
        for r in range(nr):
            p.on("scalar", lambda e, h=h, r=r: e.activation(
                out=stg2[0][:, r * 512:(r + 1) * 512], in_=PS[r][:, :],
                func=AF.Sigmoid, bias=bg2c[:, h:h + 1]), inc=(cm.s, 1))
        p.wait("sync", cm.s)
        p.on("sync", lambda e, h=h: e.dma_start(
            out=gate_s[h * 128:(h + 1) * 128, :], in_=stg2[0][:, 0:TT]),
            inc=(st_g2, 16))
        p.wait("scalar", st_g2)
    p.barrier()

    if phases == 13:
        p.barrier()
        return p.build()
    bst = p.sem("bst")
    qk_specs = []
    for h in range(NH):
        qk_specs.append(("q", h, wqt_d, h * 128, cqd_d, h * 4 * 128, q_s, h * 128, None))
    for h in range(NH):
        for i in range(4):
            gi = h * 4 + i
            qk_specs.append(("k", gi, wkt_d, gi * 128, ckd_d, gi * 4 * 128,
                             k_s, gi * 128, gi))
    for h in range(NH):
        for i in range(4):
            gi = h * 4 + i
            qk_specs.append(("v", 2 * NH * 4 + gi, wvt_d, gi * 128, cvd_d,
                             gi * 4 * 128, None, gi, None))
    ps_tb = PS[6][:, :].bitcast(BF16)
    if qkn is not None:
        qk_specs = qk_specs[:qkn]
    for kind, g, wt_src, wcol, cd_src, crow, dst_s, drow, brow in qk_specs:
        wl = wload[g % 2]
        wbb = wb16[g % 2]
        cdgb = cdg[g % 2]
        for half in range(2):
            p.on("gpsimd", lambda e, wl=wl, half=half, wt_src=wt_src, wcol=wcol: e.dma_start(
                out=wl[:, :].rearrange("p (c f) -> p c f", f=128),
                in_=wt_src[:, wcol:wcol + 128].rearrange(
                    "(c p) f -> p c f", p=128)[:, half * 8:(half + 1) * 8, :]),
                wait=[(cm.t, cm.t.c)], inc=(dwb[g % 2], 16))
            p.wait("vector", dwb[g % 2])
            p.on("vector", lambda e, wl=wl, wbb=wbb, half=half: e.tensor_copy(
                out=wbb[:, :], in_=wl[:, :]), wait=[(cm.t, cm.t.c)], inc=(cm.v, 1))
            p.wait("tensor", cm.v)
            for cj in range(8):
                ci = half * 8 + cj
                for r in range(nr):
                    p.on("tensor", lambda e, wbb=wbb, cj=cj, ci=ci, r=r: e.matmul(
                        PS[r][:, :], wbb[:, cj * 128:(cj + 1) * 128],
                        xn[ci][:, r * 512:(r + 1) * 512],
                        start=(ci == 0), stop=(ci == NCH - 1)), inc=(cm.t, 1))
        if qkstage != 16:
            p.on("gpsimd", lambda e, cd_src=cd_src, crow=crow: e.dma_start(
                out=wload[0][:, 0:512].rearrange("p (j f) -> p j f", f=128),
                in_=cd_src[crow:crow + 512, :].rearrange("(j p) f -> p j f", p=128)),
                wait=[(cm.v, cm.v.c)], inc=(dcon, 16))
            p.wait("vector", dcon)
        p.on("vector", lambda e, cdgb=cdgb: e.tensor_copy(out=cdgb[:, :],
                                                          in_=wload[0][:, 0:512]),
             wait=[(cm.t, cm.t.c)], inc=(cm.v, 1))
        sg = stg[g % 2]
        p.on("vector", lambda e, sg=sg: e.memset(sg[:, 0:4], 0.0),
             wait=[(stpB[g % 2], lastB[g % 2])], inc=(cm.v, 1))
        p.wait("scalar", cm.t)
        p.wait("scalar", cm.v)
        for r in range(nr):
            p.on("scalar", lambda e, sg=sg, r=r: e.activation(
                out=sg[:, 4 + r * 512:4 + (r + 1) * 512], in_=PS[r][:, :],
                func=AF.Identity), inc=(cm.s, 1))
        if qkstage < 2:
            continue
        qs2 = qkstage
        p.wait("tensor", cm.s)
        sq2 = stg2[g % 2]
        for r in range(nr):
            for j in range(4):
                p.on("tensor", lambda e, sg=sg, cdgb=cdgb, r=r, j=j: e.matmul(
                    PS[4][:, :], cdgb[:, j * 128:(j + 1) * 128],
                    sg[:, 1 + j + r * 512:1 + j + r * 512 + 512],
                    start=(j == 0), stop=(j == 3)),
                    wait=(([(cm.v, cm.v.c)] if r == 0 else []) +
                          [(cm.s, cm.s.c)] if j == 0 else ()),
                    inc=(cm.t, 1))
            p.wait("scalar", cm.t)
            p.on("scalar", lambda e, sq2=sq2, r=r: e.activation(
                out=sq2[:, r * 512:(r + 1) * 512], in_=PS[4][:, :],
                func=(AF.Silu if qkstage != 17 else AF.Identity)),
                wait=([(stpA[g % 2], lastA[g % 2])] if r == 0 else ()),
                inc=(cm.s, 1))
        if qkstage < 3:
            continue
        if kind == "v":
            hh, ii = drow // 4, drow % 4
            for tt in range(NTT):
                p.on("tensor", lambda e, sq2=sq2, tt=tt: e.transpose(
                    ps_tb[0:128, 0:128], sq2[:, tt * 128:(tt + 1) * 128],
                    cm.identb[:, :]),
                    wait=[(cm.s, cm.s.c)] if tt == 0 else [(cm.v, cm.v.c)],
                    inc=(cm.t, 1))
                p.wait("vector", cm.t)
                p.on("vector", lambda e: e.tensor_copy(out=vtb[:, :],
                                                       in_=ps_tb[0:128, 0:128]),
                     wait=[(stv2, lastV[0])], inc=(cm.v, 1))
                p.wait("sync", cm.v)
                p.on("sync", lambda e, tt=tt, hh=hh, ii=ii: e.dma_start(
                    out=v_s.rearrange("(t f) c -> f t c", f=4)[
                        ii, tt * 128:(tt + 1) * 128, hh * 128:(hh + 1) * 128],
                    in_=vtb[:, :]), inc=(stv2, 16))
                lastV[0] = stv2.c
            continue
        p.wait("scalar", cm.s)
        for r in range(nr):
            p.on("scalar", lambda e, sq2=sq2, r=r: e.activation(
                out=sqb_l2[:, 0:512],
                in_=sq2[:, r * 512:(r + 1) * 512], func=AF.Square),
                wait=[(cm.t, cm.t.c)], inc=(cm.s, 1))
            p.wait("tensor", cm.s)
            p.on("tensor", lambda e, r=r: e.matmul(
                PS[5][0:1, 0:512],
                onesb_d[:, :], sqb_l2[:, 0:512], start=True, stop=True),
                wait=[(cm.v, cm.v.c)], inc=(cm.t, 1))
            p.wait("vector", cm.t)
            p.on("vector", lambda e, r=r: e.tensor_copy(
                out=rowbuf[0:1, r * 512:(r + 1) * 512], in_=PS[5][0:1, 0:512]),
                inc=(cm.v, 1))
        p.on("vector", lambda e: e.tensor_single_scalar(
            out=rowbuf[0:1, 0:TT], in_=rowbuf[0:1, 0:TT], scalar=1e-24, op=OP.add),
            inc=(cm.v, 1))
        p.wait("scalar", cm.v)
        p.on("scalar", lambda e: e.activation(out=rowbuf[0:1, 0:TT],
                                              in_=rowbuf[0:1, 0:TT], func=AF.Ln))
        p.on("scalar", lambda e: e.activation(out=rowbuf[0:1, 0:TT],
                                              in_=rowbuf[0:1, 0:TT], func=AF.Exp,
                                              scale=-0.5), inc=(cm.s, 1))
        for r in range(nr):
            p.on("tensor", lambda e, r=r: e.matmul(
                PS[5][:, :], cm.ones_row[:, :], rowbuf[0:1, r * 512:(r + 1) * 512],
                start=True, stop=True),
                wait=([(cm.s, cm.s.c)] if r == 0 else []) + [(cm.v, cm.v.c)],
                inc=(cm.t, 1))
            p.wait("vector", cm.t)
            p.on("vector", lambda e, sq2=sq2, r=r: e.tensor_tensor(
                out=sq2[:, r * 512:(r + 1) * 512], in0=sq2[:, r * 512:(r + 1) * 512],
                in1=PS[5][:, :], op=OP.mult), inc=(cm.v, 1))
        p.wait("sync", cm.v)
        p.on("sync", lambda e, sq2=sq2, dst_s=dst_s, drow=drow: e.dma_start(
            out=dst_s[drow:drow + 128, :], in_=sq2[:, 0:TT]), inc=(stpA[g % 2], 16))
        lastA[g % 2] = stpA[g % 2].c
        if brow is not None:
            p.on("gpsimd", lambda e, brow=brow: e.dma_start(
                out=rowbuf[0:1, 0:TT], in_=beta_s[brow:brow + 1, :]),
                wait=[(cm.t, cm.t.c), (stp[0], 16)], inc=(bst, 16))
            p.wait("tensor", bst)
            for r in range(nr):
                p.on("tensor", lambda e, r=r: e.matmul(
                    PS[5][:, :], cm.ones_row[:, :], rowbuf[0:1, r * 512:(r + 1) * 512],
                    start=True, stop=True), wait=[(cm.v, cm.v.c)], inc=(cm.t, 1))
                p.wait("vector", cm.t)
                p.on("vector", lambda e, sg=sg, sq2=sq2, r=r: e.tensor_tensor(
                    out=sg[:, r * 512:(r + 1) * 512], in0=sq2[:, r * 512:(r + 1) * 512],
                    in1=PS[5][:, :], op=OP.mult), inc=(cm.v, 1))
            p.wait("sync", cm.v)
            p.on("sync", lambda e, sg=sg, drow=drow: e.dma_start(
                out=kb_s[drow:drow + 128, :], in_=sg[:, 0:TT]), inc=(stpB[g % 2], 16))
            lastB[g % 2] = stpB[g % 2].c
    p.barrier()

    if phases == 1:
        for sem_ in (stpA[0], stpA[1], stpB[0], stpB[1], stv2, st_g, st_g2, stp[0]):
            p.wait("gpsimd", sem_)
            p.wait("sync", sem_)
        return p.build()
    kwin = [cv(0, 8192), cv(16384, 8192)]
    kbwin = [cv(32768, 8192), cv(49152, 8192)]
    Gwin = [cv(65536, 2048, F32), cv(73728, 2048, F32)]
    qwin = [cv(81920, 2048), cv(86016, 2048)]
    gatewin = [cv(90112, 2048), cv(94208, 2048)]
    vci = [cv(98304, 1024), cv(100352, 1024)]
    khs = cv(102400, 1024)
    whs = cv(104448, 1024)
    kht = cv(106496, 512)
    wht = cv(107520, 512)
    kexpt = cv(108544, 1024)
    kdect = cv(110592, 1024)
    qhs = cv(112640, 256)
    qht = cv(113152, 128)
    qhatt = cv(113664, 256)
    MT_sb = cv(114176, 1024)
    M_sb = cv(116224, 1024)
    Dn_sb = cv(118272, 1024)
    DnT_sb = cv(120320, 1024)
    Parch = [cv(122368 + i * 2048, 1024) for i in range(4)]
    Prot = [cv(130560, 1024), cv(132608, 1024)]
    Z_sb = [cv(134656, 1024), cv(136704, 1024)]
    y_sb = cv(138752, 1024)
    u_sb = cv(140800, 1024)
    attnT_sb = cv(142848, 1024)
    kdT_sb = cv(144896, 1024)
    S_f = cv(146944, 1024, F32)
    Sb_t = cv(151040, 1024)
    oar = cv(153088, 256, F32)
    ex1 = cv(154112, 256, F32)
    ex2 = cv(155136, 256, F32)
    ex3 = cv(156160, 128, F32)
    ex4 = cv(156672, 128, F32)
    sm = [cv(157184 + i * 64, 16, F32) for i in range(16)]
    ow = [cv(159232, 2048), cv(163328, 2048)]
    gsl = cv(167424, 256, F32)
    sqog = cv(168448, 256)
    MTo_sb = cv(169472, 1024)
    mmt = p.sb("mmt", [128, 512], U8)
    mmu = p.sb("mmu", [128, 512], U8)
    amk = p.sb("amk", [128, 256], U8)
    dmk = p.sb("dmk", [128, 1024], U8)
    dmkT = p.sb("dmkT", [128, 1024], U8)
    zb = p.sb("zb", [128, 1024], BF16)
    mmt_d = p.dram("mmt_in", [128, 512], U8, kind="ExternalInput").ap()
    mmu_d = p.dram("mmu_in", [128, 512], U8, kind="ExternalInput").ap()
    amk_d = p.dram("amk_in", [128, 256], U8, kind="ExternalInput").ap()
    dmk_d = p.dram("dmk_in", [128, 1024], U8, kind="ExternalInput").ap()
    dmkT_d = p.dram("dmkT_in", [128, 1024], U8, kind="ExternalInput").ap()
    dmsk = p.sem("dmsk")
    p.on("sync", lambda e: e.dma_start(out=mmt[:, :], in_=mmt_d), inc=(dmsk, 16))
    p.on("sync", lambda e: e.dma_start(out=mmu[:, :], in_=mmu_d), inc=(dmsk, 16))
    p.on("sync", lambda e: e.dma_start(out=amk[:, :], in_=amk_d), inc=(dmsk, 16))
    p.on("sync", lambda e: e.dma_start(out=dmk[:, :], in_=dmk_d), inc=(dmsk, 16))
    p.on("sync", lambda e: e.dma_start(out=dmkT[:, :], in_=dmkT_d), inc=(dmsk, 16))
    msk_ld = dmsk.c
    p.wait("vector", dmsk, msk_ld)
    p.on("vector", lambda e: e.memset(zb[:, :], 0.0))
    p.on("vector", lambda e: e.memset(S_f, 0.0))
    p.on("vector", lambda e: e.memset(Sb_t, 0.0), inc=(cm.v, 1))

    dwin = [p.sem("dwin0"), p.sem("dwin1")]
    dvs = [p.sem("dvs0"), p.sem("dvs1")]
    stow = [p.sem("stow0"), p.sem("stow1")]
    NW = NCI // 8

    def load_window(w):
        wb = w % 2
        t0 = w * 256
        wt = [(cm.v, cm.v.c)]
        for part in range(4):
            p.on("gpsimd", lambda e, wb=wb, t0=t0, part=part: e.dma_start(
                out=kwin[wb].rearrange("p (g t) -> p g t", t=256)[:, part * 8:(part + 1) * 8, :],
                in_=k_s.rearrange("(g p) t -> p g t", p=128)[:, part * 8:(part + 1) * 8,
                                                            t0:t0 + 256]),
                wait=(wt if part == 0 else ()), inc=(dwin[wb], 16))
            p.on("gpsimd", lambda e, wb=wb, t0=t0, part=part: e.dma_start(
                out=kbwin[wb].rearrange("p (g t) -> p g t", t=256)[:, part * 8:(part + 1) * 8, :],
                in_=kb_s.rearrange("(g p) t -> p g t", p=128)[:, part * 8:(part + 1) * 8,
                                                              t0:t0 + 256]),
                inc=(dwin[wb], 16))
        p.on("gpsimd", lambda e, wb=wb, t0=t0: e.dma_start(
            out=Gwin[wb].rearrange("p (h t) -> p h t", t=256),
            in_=g_s.rearrange("(h p) t -> p h t", p=128)[:, :, t0:t0 + 256]),
            inc=(dwin[wb], 16))
        p.on("gpsimd", lambda e, wb=wb, t0=t0: e.dma_start(
            out=qwin[wb].rearrange("p (h t) -> p h t", t=256),
            in_=q_s.rearrange("(h p) t -> p h t", p=128)[:, :, t0:t0 + 256]),
            inc=(dwin[wb], 16))
        p.on("gpsimd", lambda e, wb=wb, t0=t0: e.dma_start(
            out=gatewin[wb].rearrange("p (h t) -> p h t", t=256),
            in_=gate_s.rearrange("(h p) t -> p h t", p=128)[:, :, t0:t0 + 256]),
            inc=(dwin[wb], 16))

    for sem_ in (stpA[0], stpA[1], stpB[0], stpB[1], stv2, st_g, st_g2, stp[0]):
        p.wait("gpsimd", sem_)
        p.wait("sync", sem_)
    load_window(0)
    for ci in range(2):
        p.on("sync", lambda e, ci=ci: e.dma_start(
            out=vci[ci % 2][0:128, :], in_=v_s[ci * 128:(ci + 1) * 128, :]),
            inc=(dvs[ci % 2], 16))

    dbg_outs = {}
    if dbg:
        for nm, cols, dt_ in [("d_khs", 1024, BF16), ("d_whs", 1024, BF16),
                              ("d_kht", 512, BF16), ("d_wht", 512, BF16),
                              ("d_MT", 1024, BF16), ("d_M", 1024, BF16),
                              ("d_Dn", 1024, BF16), ("d_Z", 1024, BF16),
                              ("d_y", 1024, BF16), ("d_u", 1024, BF16),
                              ("d_aT", 256, BF16), ("d_kexp", 1024, BF16),
                              ("d_ex1", 256, F32), ("d_sg", 16, F32),
                              ("d_oar", 256, F32), ("d_qhatt", 256, BF16)]:
            dbg_outs[nm] = (p.dram(nm, [128, cols], dt_, kind="ExternalOutput").ap(), cols, dt_)
    dbsem = p.sem("dbsem")

    def dspill(nm, src_ap):
        if not dbg:
            return
        p.wait("sync", cm.v)
        p.wait("sync", cm.t)
        p.wait("sync", cm.s)
        p.on("sync", lambda e, nm=nm, src_ap=src_ap: e.dma_start(
            out=dbg_outs[nm][0], in_=src_ap), inc=(dbsem, 16))
        p.wait("sync", dbsem)
        p.wait("vector", dbsem)
        p.wait("scalar", dbsem)
        p.wait("tensor", dbsem)
        p.wait("gpsimd", dbsem)

    onwc = p.sb("onwc", [128, 1], F32)
    donw = p.sem("donw")
    p.on("sync", lambda e: e.dma_start(out=onwc[:, :], in_=onw_d), inc=(donw, 16))
    p.wait("vector", donw)
    SCALE_O = float(128 ** -0.5)

    def emit_ci(ci):
        w = ci // 8
        wb = w % 2
        ciw = ci % 8
        cib = ci % 2
        if ciw == 0:
            p.wait("tensor", dwin[wb])
            p.wait("vector", dwin[wb])
            p.wait("scalar", dwin[wb])
            if w + 1 < NW:
                load_window(w + 1)
        p.wait("vector", dvs[cib])

        Gc = Gwin[wb].rearrange("p (h t) -> p h t", t=256)[:, :, ciw * 32:(ciw + 1) * 32]
        kv = kwin[wb].rearrange("p (h i t) -> p h i t", i=4, t=256)[
            :, :, :, ciw * 32:(ciw + 1) * 32]
        kbv = kbwin[wb].rearrange("p (h i t) -> p h i t", i=4, t=256)[
            :, :, :, ciw * 32:(ciw + 1) * 32]
        qv = qwin[wb].rearrange("p (h t) -> p h t", t=256)[:, :, ciw * 32:(ciw + 1) * 32]

        V = lambda f, inc=None, wait=(): p.on("vector", f, wait=wait, inc=inc)
        S = lambda f, inc=None, wait=(): p.on("scalar", f, wait=wait, inc=inc)
        T = lambda f, inc=None, wait=(): p.on("tensor", f, wait=wait, inc=inc)
        V(lambda e: e.tensor_copy(out=sm[0][:, 0:8], in_=Gc[:, :, 15]))
        V(lambda e: e.tensor_copy(out=sm[1][:, 0:8], in_=Gc[:, :, 31]))
        V(lambda e: e.tensor_single_scalar(
            out=sm[2][:, :].rearrange("p (h j) -> p h j", j=2)[:, :, 0],
            in_=sm[0][:, 0:8], scalar=0.5, op=OP.mult))
        V(lambda e: e.tensor_tensor(out=sm[6][:, 0:8], in0=sm[0][:, 0:8],
                                    in1=sm[1][:, 0:8], op=OP.add))
        V(lambda e: e.tensor_single_scalar(
            out=sm[2][:, :].rearrange("p (h j) -> p h j", j=2)[:, :, 1],
            in_=sm[6][:, 0:8], scalar=0.5, op=OP.mult))
        V(lambda e: e.tensor_tensor(
            out=ex1[:, :].rearrange("p (h j t) -> p h j t", j=2, t=16),
            in0=Gc.rearrange("p h (j t) -> p h j t", j=2),
            in1=sm[2][:, :].rearrange("p (h j) -> p h j", j=2).unsqueeze(3)
                .broadcast_to([128, 8, 2, 16]),
            op=OP.subtract))
        V(lambda e: e.tensor_scalar(out=ex1[:, :], in0=ex1[:, :], scalar1=85.0,
                                    scalar2=-85.0, op0=OP.min, op1=OP.max),
          inc=(cm.v, 1))
        p.wait("scalar", cm.v)
        S(lambda e: e.activation(out=ex2[:, :], in_=ex1[:, :], func=AF.Exp, scale=-1.0))
        S(lambda e: e.activation(out=ex1[:, :], in_=ex1[:, :], func=AF.Exp))
        p.wait("vector", cm.s)
        V(lambda e: e.tensor_tensor(
            out=ex3[:, :].rearrange("p (h t) -> p h t", t=16),
            in0=Gc[:, :, 16:32],
            in1=sm[0][:, 0:8].unsqueeze(2).broadcast_to([128, 8, 16]),
            op=OP.subtract))
        V(lambda e: e.tensor_tensor(
            out=ex4[:, :].rearrange("p (h t) -> p h t", t=16),
            in0=Gc[:, :, 0:16],
            in1=sm[0][:, 0:8].unsqueeze(2).broadcast_to([128, 8, 16]),
            op=OP.subtract))
        V(lambda e: e.tensor_tensor(
            out=sm[7][:, :].rearrange("p (h j) -> p h j", j=2),
            in0=sm[1][:, 0:8].unsqueeze(2).broadcast_to([128, 8, 2]),
            in1=sm[2][:, :].rearrange("p (h j) -> p h j", j=2), op=OP.subtract),
          inc=(cm.v, 1))
        p.wait("scalar", cm.v)
        S(lambda e: e.activation(out=ex3[:, :], in_=ex3[:, :], func=AF.Exp))
        S(lambda e: e.activation(out=ex4[:, :], in_=ex4[:, :], func=AF.Exp, scale=-1.0))
        S(lambda e: e.activation(out=sm[3][:, :], in_=sm[2][:, :], func=AF.Exp))
        S(lambda e: e.activation(out=sm[4][:, :], in_=sm[7][:, :], func=AF.Exp))
        S(lambda e: e.activation(out=sm[5][:, 0:8], in_=sm[1][:, 0:8], func=AF.Exp),
          inc=(cm.s, 1))
        p.wait("vector", cm.s)

        e1b = ex1[:, :].rearrange("p (h t) -> p h t", t=32).unsqueeze(3).broadcast_to([128, 8, 32, 4])
        e2b = ex2[:, :].rearrange("p (h t) -> p h t", t=32).unsqueeze(3).broadcast_to([128, 8, 32, 4])
        kvt = kv.transpose([0, 1, 3, 2])
        kbvt = kbv.transpose([0, 1, 3, 2])
        V(lambda e: e.tensor_tensor(out=khs[:, :].rearrange("p (h t i) -> p h t i", t=32, i=4),
                                    in0=kvt, in1=e1b, op=OP.mult))
        V(lambda e: e.tensor_tensor(out=whs[:, :].rearrange("p (h t i) -> p h t i", t=32, i=4),
                                    in0=kbvt, in1=e2b, op=OP.mult))
        e3b = ex3[:, :].rearrange("p (h t) -> p h t", t=16).unsqueeze(3).broadcast_to([128, 8, 16, 4])
        e4b = ex4[:, :].rearrange("p (h t) -> p h t", t=16).unsqueeze(3).broadcast_to([128, 8, 16, 4])
        V(lambda e: e.tensor_tensor(out=kht[:, :].rearrange("p (h t i) -> p h t i", t=16, i=4),
                                    in0=kv[:, :, :, 16:32].transpose([0, 1, 3, 2]),
                                    in1=e3b, op=OP.mult))
        V(lambda e: e.tensor_tensor(out=wht[:, :].rearrange("p (h t i) -> p h t i", t=16, i=4),
                                    in0=kbv[:, :, :, 0:16].transpose([0, 1, 3, 2]),
                                    in1=e4b, op=OP.mult))
        egsb = sm[3][:, :].rearrange("p (h j) -> p h j", j=2).unsqueeze(3).broadcast_to([128, 8, 2, 64])
        egdb = sm[4][:, :].rearrange("p (h j) -> p h j", j=2).unsqueeze(3).broadcast_to([128, 8, 2, 64])
        V(lambda e: e.tensor_tensor(out=kexpt[:, :].rearrange("p (h j x) -> p h j x", j=2, x=64),
                                    in0=khs[:, :].rearrange("p (h j x) -> p h j x", j=2, x=64),
                                    in1=egsb, op=OP.mult))
        V(lambda e: e.tensor_tensor(out=kdect[:, :].rearrange("p (h j x) -> p h j x", j=2, x=64),
                                    in0=whs[:, :].rearrange("p (h j x) -> p h j x", j=2, x=64),
                                    in1=egdb, op=OP.mult))
        V(lambda e: e.tensor_tensor(out=qhs[:, :].rearrange("p (h t) -> p h t", t=32),
                                    in0=qv,
                                    in1=ex1[:, :].rearrange("p (h t) -> p h t", t=32),
                                    op=OP.mult))
        V(lambda e: e.tensor_tensor(out=qht[:, :].rearrange("p (h t) -> p h t", t=16),
                                    in0=qv[:, :, 16:32],
                                    in1=ex3[:, :].rearrange("p (h t) -> p h t", t=16),
                                    op=OP.mult))
        egsq = sm[3][:, :].rearrange("p (h j) -> p h j", j=2).unsqueeze(3).broadcast_to([128, 8, 2, 16])
        V(lambda e: e.tensor_tensor(out=qhatt[:, :].rearrange("p (h j t) -> p h j t", j=2, t=16),
                                    in0=qhs[:, :].rearrange("p (h j t) -> p h j t", j=2, t=16),
                                    in1=egsq, op=OP.mult), inc=(cm.v, 1))
        resc_done = cm.v.c
        if ci == 0:
            dspill("d_khs", khs[:, :])
            dspill("d_whs", whs[:, :])
            dspill("d_kht", kht[:, :])
            dspill("d_wht", wht[:, :])
            dspill("d_kexp", kexpt[:, :])
            dspill("d_ex1", ex1[:, :])
            dspill("d_sg", sm[2][:, :])

        if ci_stage < 1:
            return
        p.wait("tensor", cm.v, resc_done)
        for h in range(NH):
            g4 = h // 4
            hc = (h % 4) * 128
            T(lambda e, h=h, g4=g4, hc=hc: e.matmul(
                PS[g4][:, hc:hc + 128], whs[:, h * 128:(h + 1) * 128],
                khs[:, h * 128:(h + 1) * 128], start=True, stop=True), inc=(cm.t, 1))
            T(lambda e, h=h, g4=g4, hc=hc: e.matmul(
                PS[g4][0:64, hc + 64:hc + 128], wht[:, h * 64:(h + 1) * 64],
                kht[:, h * 64:(h + 1) * 64], start=True, stop=True), inc=(cm.t, 1))
            T(lambda e, h=h, g4=g4, hc=hc: e.matmul(
                PS[2 + g4][:, hc:hc + 128], khs[:, h * 128:(h + 1) * 128],
                whs[:, h * 128:(h + 1) * 128], start=True, stop=True), inc=(cm.t, 1))
            T(lambda e, h=h, g4=g4, hc=hc: e.matmul(
                PS[2 + g4][64:128, hc:hc + 64], kht[:, h * 64:(h + 1) * 64],
                wht[:, h * 64:(h + 1) * 64], start=True, stop=True,
                tile_position=(0, 64)), inc=(cm.t, 1))
        p.wait("vector", cm.t)
        for g4 in range(2):
            V(lambda e, g4=g4: e.select(
                out=MT_sb[:, g4 * 512:(g4 + 1) * 512], mask=mmt[:, :],
                on_true=PS[g4][:, :], on_false=zb[:, 0:512], add_drain=True))
            V(lambda e, g4=g4: e.select(
                out=M_sb[:, g4 * 512:(g4 + 1) * 512], mask=mmu[:, :],
                on_true=PS[2 + g4][:, :], on_false=zb[:, 0:512], add_drain=True),
              inc=(cm.v, 1))
        V(lambda e: e.select(out=Dn_sb[:, :], mask=dmk[:, :], on_true=M_sb[:, :],
                             on_false=zb[:, :], add_drain=True))
        V(lambda e: e.tensor_single_scalar(out=Dn_sb[:, :], in_=Dn_sb[:, :],
                                           scalar=-1.0, op=OP.mult))
        V(lambda e: e.select(out=DnT_sb[:, :], mask=dmkT[:, :], on_true=MT_sb[:, :],
                             on_false=zb[:, :], add_drain=True))
        V(lambda e: e.tensor_single_scalar(out=DnT_sb[:, :], in_=DnT_sb[:, :],
                                           scalar=-1.0, op=OP.mult), inc=(cm.v, 1))
        V(lambda e: e.select(out=MTo_sb[:, :], mask=dmkT[:, :], on_true=zb[:, :],
                             on_false=MT_sb[:, :], add_drain=True), inc=(cm.v, 1))
        dn_done = cm.v.c
        if ci == 0:
            dspill("d_MT", MT_sb[:, :])
            dspill("d_M", M_sb[:, :])
            dspill("d_Dn", Dn_sb[:, :])

        if ci_stage < 2:
            return
        cur_p, cur_t = Dn_sb, DnT_sb
        for rnd in range(4):
            p.wait("tensor", cm.v)
            for h in range(NH):
                g4 = h // 4
                hc = (h % 4) * 128
                T(lambda e, h=h, g4=g4, hc=hc, cur_p=cur_p, cur_t=cur_t: e.matmul(
                    PS[g4][:, hc:hc + 128], cur_t[:, h * 128:(h + 1) * 128],
                    cur_p[:, h * 128:(h + 1) * 128], start=True, stop=True),
                    inc=(cm.t, 1))
                T(lambda e, h=h, g4=g4, hc=hc, cur_p=cur_p, cur_t=cur_t: e.matmul(
                    PS[2 + g4][:, hc:hc + 128], cur_p[:, h * 128:(h + 1) * 128],
                    cur_t[:, h * 128:(h + 1) * 128], start=True, stop=True),
                    inc=(cm.t, 1))
            p.wait("scalar", cm.t)
            nxt_p = Parch[rnd]
            nxt_t = Prot[rnd % 2]
            for g4 in range(2):
                S(lambda e, g4=g4, nxt_p=nxt_p: e.activation(
                    out=nxt_p[:, g4 * 512:(g4 + 1) * 512], in_=PS[g4][:, :],
                    func=AF.Identity))
                S(lambda e, g4=g4, nxt_t=nxt_t: e.activation(
                    out=nxt_t[:, g4 * 512:(g4 + 1) * 512], in_=PS[2 + g4][:, :],
                    func=AF.Identity), inc=(cm.s, 1))
            p.wait("tensor", cm.s)
            p.wait("vector", cm.s)
            cur_p, cur_t = nxt_p, nxt_t

        if ci_stage < 3:
            return
        V(lambda e: e.tensor_tensor(
            out=Z_sb[0][:, :].rearrange("p (h f) -> p h f", f=128),
            in0=DnT_sb[:, :].rearrange("p (h f) -> p h f", f=128),
            in1=cm.identb[:, :].unsqueeze(1).broadcast_to([128, 8, 128]), op=OP.add),
            inc=(cm.v, 1))
        zcur = Z_sb[0]
        if ci_stage == 31:
            return
        for st in range(1 if ci_stage in (32, 33, 34) else 4):
            p.wait("tensor", cm.v if st == 0 else cm.s)
            for h in range(NH):
                g4 = h // 4
                hc = (h % 4) * 128
                if ci_stage != 33:
                    T(lambda e, h=h, g4=g4, hc=hc, zcur=zcur: e.matmul(
                        PS[g4][:, hc:hc + 128], cm.identb[:, :],
                        zcur[:, h * 128:(h + 1) * 128], start=True,
                        stop=(ci_stage == 34)), inc=(cm.t, 1))
                if ci_stage != 34:
                    T(lambda e, h=h, g4=g4, hc=hc, zcur=zcur, st=st: e.matmul(
                        PS[g4][:, hc:hc + 128], Parch[st][:, h * 128:(h + 1) * 128],
                        zcur[:, h * 128:(h + 1) * 128], start=(ci_stage == 33),
                        stop=True), inc=(cm.t, 1))
            p.wait("scalar", cm.t)
            znxt = Z_sb[(st + 1) % 2]
            for g4 in range(2):
                S(lambda e, g4=g4, znxt=znxt: e.activation(
                    out=znxt[:, g4 * 512:(g4 + 1) * 512], in_=PS[g4][:, :],
                    func=AF.Identity), inc=(cm.s, 1))
            zcur = znxt
        wt_final = zcur
        p.wait("tensor", cm.s)
        p.wait("vector", cm.s)
        if ci == 0:
            dspill("d_Z", wt_final[:, :])

        if ci_stage < 4 or ci_stage in (32, 33, 34):
            return
        _wl = 1 if ci_stage in (51, 52, 53, 54) else (2 if ci_stage == 55 else (3 if ci_stage == 56 else 4))
        for J in range(_wl):
            p.barrier()
            for h in range(NH):
                g4 = h // 4
                hc = (h % 4) * 128
                T(lambda e, h=h, g4=g4, hc=hc, J=J: e.matmul(
                    PS[2 + g4][J * 32:(J + 1) * 32, hc:hc + 128],
                    kexpt[:, h * 128 + J * 32:h * 128 + (J + 1) * 32],
                    Sb_t[:, h * 128:(h + 1) * 128],
                    start=True, stop=(J == 0 or ci_stage == 57),
                    tile_position=(0, J * 32)), inc=(cm.t, 1))
                if J > 0 and ci_stage != 57:
                    T(lambda e, h=h, g4=g4, hc=hc, J=J: e.matmul(
                        PS[2 + g4][J * 32:(J + 1) * 32, hc:hc + 128],
                        MTo_sb[:, h * 128 + J * 32:h * 128 + (J + 1) * 32],
                        u_sb[:, h * 128:(h + 1) * 128],
                        start=False, stop=True,
                        tile_position=(0, J * 32)), inc=(cm.t, 1))
            if ci_stage == 51:
                continue
            p.wait("vector", cm.t)
            for g4 in range(2):
                V(lambda e, g4=g4, J=J, cib=cib: e.tensor_tensor(
                    out=y_sb[J * 32:(J + 1) * 32, g4 * 512:(g4 + 1) * 512],
                    in0=vci[cib][J * 32:(J + 1) * 32, g4 * 512:(g4 + 1) * 512],
                    in1=PS[2 + g4][J * 32:(J + 1) * 32, :], op=OP.subtract),
                    inc=(cm.v, 1))
            if ci_stage == 52:
                continue
            p.wait("tensor", cm.v)
            for h in range(NH):
                g4 = h // 4
                hc = (h % 4) * 128
                T(lambda e, h=h, g4=g4, hc=hc, J=J, wt_final=wt_final: e.matmul(
                    (PS[4] if g4 == 0 else PS[6])[J * 32:(J + 1) * 32, hc:hc + 128],
                    wt_final[J * 32:(J + 1) * 32,
                             h * 128 + J * 32:h * 128 + (J + 1) * 32],
                    y_sb[J * 32:(J + 1) * 32, h * 128:(h + 1) * 128],
                    start=True, stop=True, tile_position=(J * 32, J * 32)),
                    inc=(cm.t, 1))
            if ci_stage == 53:
                continue
            p.wait("vector", cm.t)
            for g4 in range(2):
                V(lambda e, g4=g4, J=J: e.tensor_copy(
                    out=u_sb[J * 32:(J + 1) * 32, g4 * 512:(g4 + 1) * 512],
                    in_=(PS[4] if g4 == 0 else PS[6])[J * 32:(J + 1) * 32, :]),
                    inc=(cm.v, 1))
            p.wait("tensor", cm.v)
        p.barrier()

        if ci == 0:
            dspill("d_y", y_sb[:, :])
            dspill("d_u", u_sb[:, :])
        if ci_stage < 5 or ci_stage in (51, 52, 53, 54, 55, 56, 57):
            return
        for h in range(NH):
            T(lambda e, h=h: e.matmul(
                PS[0][:, h * 32:(h + 1) * 32], whs[:, h * 128:(h + 1) * 128],
                qhs[:, h * 32:(h + 1) * 32], start=True, stop=True), inc=(cm.t, 1))
            T(lambda e, h=h: e.matmul(
                PS[0][0:64, h * 32 + 16:(h + 1) * 32], wht[:, h * 64:(h + 1) * 64],
                qht[:, h * 16:(h + 1) * 16], start=True, stop=True), inc=(cm.t, 1))
        p.wait("vector", cm.t)
        V(lambda e: e.select(
            out=attnT_sb[:, 0:256], mask=amk[:, :],
            on_true=PS[0][:, 0:256], on_false=zb[:, 0:256],
            add_drain=True), inc=(cm.v, 1))

        if ci == 0:
            dspill("d_aT", attnT_sb[:, 0:256])
        if ci_stage < 6:
            return
        p.wait("tensor", cm.v)
        for h in range(NH):
            T(lambda e, h=h: e.matmul(
                PS[5][:, h * 32:(h + 1) * 32], Sb_t[:, h * 128:(h + 1) * 128],
                qhatt[:, h * 32:(h + 1) * 32], start=True, stop=True),
                inc=(cm.t, 1))
            if ci_stage == 60:
                continue
            T(lambda e, h=h: e.matmul(
                PS[6][:, h * 32:(h + 1) * 32], u_sb[:, h * 128:(h + 1) * 128],
                attnT_sb[:, h * 32:(h + 1) * 32], start=True, stop=True),
                inc=(cm.t, 1))
        p.wait("vector", cm.t)
        V(lambda e: e.tensor_copy(out=oar[:, :], in_=PS[5][:, 0:256]))
        if ci_stage != 60:
            V(lambda e: e.tensor_tensor(out=oar[:, :], in0=oar[:, :],
                                        in1=PS[6][:, 0:256], op=OP.add))
        V(lambda e: e.nop(), inc=(cm.v, 1))
        if ci_stage == 60:
            return
        if ci == 0:
            dspill("d_oar", oar[:, :])
            dspill("d_qhatt", qhatt[:, :])
        if ci_stage == 61:
            return
        p.wait("scalar", cm.v)
        S(lambda e: e.activation(out=sqog[:, :], in_=oar[:, :], func=AF.Square),
          inc=(cm.s, 1))
        p.wait("tensor", cm.s)
        T(lambda e: e.matmul(PS[0][0:1, 0:256], onesb_d[:, :], sqog[:, :],
                             start=True, stop=True), inc=(cm.t, 1))
        p.wait("vector", cm.t)
        V(lambda e: e.tensor_single_scalar(out=rowbuf[0:1, 0:256],
                                           in_=PS[0][0:1, 0:256],
                                           scalar=1.0 / (128.0 * 128.0), op=OP.mult))
        V(lambda e: e.tensor_single_scalar(out=rowbuf[0:1, 0:256],
                                           in_=rowbuf[0:1, 0:256],
                                           scalar=1e-5, op=OP.add), inc=(cm.v, 1))
        if ci_stage == 62:
            return
        p.wait("scalar", cm.v)
        S(lambda e: e.activation(out=rowbuf[0:1, 0:256], in_=rowbuf[0:1, 0:256],
                                 func=AF.Ln))
        S(lambda e: e.activation(out=rowbuf[0:1, 0:256], in_=rowbuf[0:1, 0:256],
                                 func=AF.Exp, scale=-0.5), inc=(cm.s, 1))
        p.wait("vector", cm.s)
        V(lambda e: e.tensor_single_scalar(out=rowbuf[0:1, 0:256],
                                           in_=rowbuf[0:1, 0:256],
                                           scalar=SCALE_O, op=OP.mult), inc=(cm.v, 1))
        if ci_stage == 63:
            return
        p.wait("tensor", cm.v)
        T(lambda e: e.matmul(PS[0][:, 256:512], cm.ones_row[:, :],
                             rowbuf[0:1, 0:256], start=True, stop=True), inc=(cm.t, 1))
        p.wait("vector", cm.t)
        V(lambda e: e.tensor_tensor(out=oar[:, :], in0=oar[:, :],
                                    in1=PS[0][:, 256:512], op=OP.mult))
        V(lambda e: e.tensor_tensor(out=oar[:, :], in0=oar[:, :],
                                    in1=onwc[:, 0:1].broadcast_to([128, 256]),
                                    op=OP.mult))
        if ci_stage == 64:
            return
        V(lambda e, wb=wb, ciw=ciw: e.tensor_copy(
            out=gsl[:, :].rearrange("p (h t) -> p h t", t=32),
            in_=gatewin[wb].rearrange("p (h t) -> p h t", t=256)[
                :, :, ciw * 32:(ciw + 1) * 32]))
        V(lambda e: e.tensor_tensor(out=oar[:, :], in0=oar[:, :], in1=gsl[:, :],
                                    op=OP.mult))
        if ci_stage == 65:
            return
        V(lambda e, wb=wb, ciw=ciw: e.tensor_copy(
            out=ow[wb].rearrange("p (h t) -> p h t", t=256)[
                :, :, ciw * 32:(ciw + 1) * 32],
            in_=oar[:, :].rearrange("p (h t) -> p h t", t=32)),
            wait=[(stow[wb], stow[wb].c)] if ciw == 0 else (), inc=(cm.v, 1))

        if ci_stage < 7:
            return
        pstb = PS[1][:, :].bitcast(BF16)
        p.wait("tensor", cm.v)
        for h in range(NH):
            T(lambda e, h=h: e.transpose(
                pstb[:, h * 128:(h + 1) * 128] if h < 4 else
                PS[3][:, :].bitcast(BF16)[:, (h - 4) * 128:(h - 3) * 128],
                kdect[:, h * 128:(h + 1) * 128], cm.identb[:, :]), inc=(cm.t, 1))
        p.wait("vector", cm.t)
        V(lambda e: e.tensor_copy(out=kdT_sb[:, 0:512], in_=pstb[:, 0:512]))
        V(lambda e: e.tensor_copy(out=kdT_sb[:, 512:1024],
                                  in_=PS[3][:, :].bitcast(BF16)[:, 0:512]),
          inc=(cm.v, 1))
        p.wait("tensor", cm.v)
        for h in range(NH):
            g4 = h // 4
            hc = (h % 4) * 128
            T(lambda e, h=h, g4=g4, hc=hc: e.matmul(
                (PS[2] if g4 == 0 else PS[4])[:, hc:hc + 128],
                kdT_sb[:, h * 128:(h + 1) * 128],
                u_sb[:, h * 128:(h + 1) * 128],
                start=True, stop=True), inc=(cm.t, 1))
        eg31b = sm[5][:, 0:8].unsqueeze(2).broadcast_to([128, 8, 128])
        V(lambda e: e.tensor_tensor(
            out=S_f[:, :].rearrange("p (h v) -> p h v", v=128), in0=S_f[:, :]
            .rearrange("p (h v) -> p h v", v=128), in1=eg31b, op=OP.mult))
        p.wait("vector", cm.t)
        V(lambda e: e.tensor_tensor(out=S_f[:, 0:512], in0=S_f[:, 0:512],
                                    in1=PS[2][:, :], op=OP.add))
        V(lambda e: e.tensor_tensor(out=S_f[:, 512:1024], in0=S_f[:, 512:1024],
                                    in1=PS[4][:, :], op=OP.add))
        V(lambda e: e.tensor_copy(out=Sb_t[:, :], in_=S_f[:, :]), inc=(cm.v, 1))

        if ci + 2 < NCI:
            p.on("sync", lambda e, ci=ci: e.dma_start(
                out=vci[ci % 2][0:128, :], in_=v_s[(ci + 2) * 128:(ci + 3) * 128, :]),
                wait=[(cm.v, cm.v.c)], inc=(dvs[ci % 2], 16))
        if ciw == 7:
            p.wait("sync", cm.v)
            for h in range(NH):
                p.on("sync", lambda e, h=h, w=w, wb=wb: e.dma_start(
                    out=out_d[h * 128:(h + 1) * 128, w * 256:(w + 1) * 256],
                    in_=ow[wb].rearrange("p (h t) -> p h t", t=256)[:, h, :]),
                    inc=(stow[wb], 16))

    if phases == 2:
        p.wait("tensor", dwin[0])
        p.wait("vector", dwin[0])
        p.wait("vector", dvs[0])
        p.wait("vector", dvs[1])
        p.wait("gpsimd", dwin[0])
        return p.build()
    for ci_ in range(NCI if nci_lim is None else nci_lim):
        emit_ci(ci_)
    p.wait("sync", stow[0])
    p.wait("sync", stow[1])
    p.wait("gpsimd", stow[0])
    p.wait("gpsimd", stow[1])
    return p.build()


def host_l1attn_inputs(x2_fm, Wq, Wk, Wv, conv_q, conv_k, conv_v, Wf1, Wf2, Wb,
                       A_log, dt_bias, Wg1, bg1, Wg2, bg2, onorm_w, role):
    f32 = np.float32
    NH = 8
    qsl = slice(role * 1024, role * 1024 + 1024)
    ksl = slice(role * 4096, role * 4096 + 4096)

    def C(a):
        return np.ascontiguousarray(a, f32)

    def diags(conv_rows):
        G = conv_rows.shape[0] // 128
        out = np.zeros((G * 4 * 128, 128), f32)
        for g in range(G):
            for j in range(4):
                blk = out[(g * 4 + j) * 128:(g * 4 + j + 1) * 128]
                np.fill_diagonal(blk, conv_rows[g * 128:(g + 1) * 128, j])
        return out

    idx = np.arange(128)
    tt_ = idx // 4
    sub = tt_ // 16
    r = idx[:, None]
    c = idx[None, :]
    mmt = (((c > r) & (sub[None, :] == sub[:, None])) |
           ((r < 64) & (c >= 64))).astype(np.uint8)
    mmu = mmt.T.copy()
    tcol = np.arange(32)[None, :]
    amk = (((tcol >= tt_[:, None]) & ((tcol // 16) == sub[:, None])) |
           ((r < 64) & (tcol >= 16))).astype(np.uint8)
    dmk = (((r // 32) == (c // 32)) & (r > c)).astype(np.uint8)
    dmkT = (((r // 32) == (c // 32)) & (r < c)).astype(np.uint8)
    return {
        "x2": C(x2_fm),
        "WqT": C(Wq[qsl].T), "WkT": C(Wk[ksl].T), "WvT": C(Wv[ksl].T),
        "cq_diag": diags(C(conv_q[qsl])),
        "ck_diag": diags(C(conv_k[ksl])),
        "cv_diag": diags(C(conv_v[ksl])),
        "Wf1T": C(Wf1.T), "Wf2T": C(Wf2[qsl].T),
        "WbT": C(Wb[role * 32:role * 32 + 32].T),
        "nAexp": C(A_log[role * 8:role * 8 + 8].reshape(1, 8)),
        "dtb_col": C(dt_bias[qsl].reshape(8, 128).T),
        "Wg1T": C(Wg1.T), "bg1_col": C(bg1.reshape(128, 1)),
        "Wg2T": C(Wg2[qsl].T), "bg2_col": C(bg2[qsl].reshape(8, 128).T),
        "onorm_col": C(onorm_w.reshape(128, 1)),
        "mmt_in": np.tile(mmt, (1, 4)), "mmu_in": np.tile(mmu, (1, 4)),
        "amk_in": np.tile(amk, (1, 8)), "dmk_in": np.tile(dmk, (1, 8)),
        "dmkT_in": np.tile(dmkT, (1, 8)),
        "ident_in": np.eye(128, dtype=f32),
    }



_PROGS = {}


def _progs():
    if not _PROGS:
        _PROGS["A"] = build_l0attn(2048)
        _PROGS["B"] = build_projlnres(1024, lora=True)
        _PROGS["C"] = build_l0ffn(1024)
        _PROGS["D"] = build_l1attn(2048)
        _PROGS["E"] = build_projlnres(1024, lora=False)
        _PROGS["F"] = build_l1ffn(1024)
    return _PROGS


def _bf16(a):
    import ml_dtypes
    if a.dtype != ml_dtypes.bfloat16:
        a = a.astype(ml_dtypes.bfloat16)
    return np.ascontiguousarray(a)


def kernel(**inp):
    P = _progs()
    cores = list(range(8))
    x = inp["L0_attn_x"]
    xT = [np.ascontiguousarray(x[b].T) for b in range(4)]

    ims = [host_l0attn_inputs(
        xT[c // 2], inp["L0_attn_Wq"], inp["L0_attn_bq"], inp["L0_attn_Bq"],
        inp["L0_attn_Aq"], inp["L0_attn_Wk"], inp["L0_attn_bk"],
        inp["L0_attn_Bk"], inp["L0_attn_Ak"], inp["L0_attn_Wv"],
        inp["L0_attn_bv"], inp["L0_attn_Bv"], inp["L0_attn_Av"], c % 2)
        for c in cores]
    res = run_bass_kernel_spmd(P["A"], ims, core_ids=cores)
    aA = [np.concatenate([res.results[2 * b]["attnout"],
                          res.results[2 * b + 1]["attnout"]], axis=0)
          for b in range(4)]

    ims = [host_projlnres_inputs(
        _bf16(aA[c // 2][:, (c % 2) * 1024:(c % 2 + 1) * 1024]),
        xT[c // 2][:, (c % 2) * 1024:(c % 2 + 1) * 1024],
        inp["L0_attn_Wo"], inp["L0_attn_bo"], inp["L0_attn_Bo"],
        inp["L0_attn_Ao"]) for c in cores]
    res = run_bass_kernel_spmd(P["B"], ims, core_ids=cores)
    x1 = [res.results[c]["xout"] for c in cores]

    ims = [host_l0ffn_inputs(x1[c], inp["L0_ffn_weight"], inp["L0_ffn_bias"])
           for c in cores]
    res = run_bass_kernel_spmd(P["C"], ims, core_ids=cores)
    x2 = [res.results[c]["x2"] for c in cores]
    x2full = [np.ascontiguousarray(np.concatenate(
        [x2[2 * b], x2[2 * b + 1]], axis=1)) for b in range(4)]

    ims = [host_l1attn_inputs(
        x2full[c // 2], inp["L1_attn_Wq"], inp["L1_attn_Wk"],
        inp["L1_attn_Wv"], inp["L1_attn_conv_q"], inp["L1_attn_conv_k"],
        inp["L1_attn_conv_v"], inp["L1_attn_Wf1"], inp["L1_attn_Wf2"],
        inp["L1_attn_Wb"], inp["L1_attn_A_log"], inp["L1_attn_dt_bias"],
        inp["L1_attn_Wg1"], inp["L1_attn_bg1"], inp["L1_attn_Wg2"],
        inp["L1_attn_bg2"], inp["L1_attn_onorm_w"], c % 2) for c in cores]
    res = run_bass_kernel_spmd(P["D"], ims, core_ids=cores)
    aD = [np.concatenate([res.results[2 * b]["ogate"],
                          res.results[2 * b + 1]["ogate"]], axis=0)
          for b in range(4)]

    ims = [host_projlnres_inputs(
        _bf16(aD[c // 2][:, (c % 2) * 1024:(c % 2 + 1) * 1024]),
        x2[c], inp["L1_attn_Wo"]) for c in cores]
    res = run_bass_kernel_spmd(P["E"], ims, core_ids=cores)
    x3 = [res.results[c]["xout"] for c in cores]

    ims = [host_l1ffn_inputs(x3[c], inp["L1_ffn_kernel"], inp["L1_ffn_bias"])
           for c in cores]
    res = run_bass_kernel_spmd(P["F"], ims, core_ids=cores)

    out = np.empty((4, 2048, 2048), np.float32)
    for c in cores:
        b, r = c // 2, c % 2
        out[b, r * 1024:(r + 1) * 1024, :] = res.results[c]["x4"].T
    return out
